# version 04
# baseline (speedup 1.0000x reference)
import math
import numpy as np
import ml_dtypes
import concourse.bass as bass
import concourse.mybir as mybir
from concourse.bass_utils import run_bass_kernel_spmd

F32 = mybir.dt.float32
BF16 = mybir.dt.bfloat16
I8 = mybir.dt.int8
AF = mybir.ActivationFunctionType
ALU = mybir.AluOpType
AX = mybir.AxisListType

D = 2048
SEQ = 2048
NSEQ = 2
NCORE = 8
TC = 1024
KT = D // 128
E = 8
NH = 16
HD = 128
R = 32
INV_SQRT_HD = 1.0 / math.sqrt(HD)
EPS_LN = 1e-5
EPS_RMS = 1.1920929e-07
ENGINES = ("sync", "tensor", "vector", "scalar", "gpsimd")


class KB:
    def __init__(self, nc):
        self.nc = nc
        self.q = {e: [] for e in ENGINES}
        self.sems = {}
        self.cnt = {}
        self.max_wait = {}
        self.is_dma = {}

    def sem(self, name):
        if name not in self.sems:
            self.sems[name] = self.nc.alloc_semaphore(name)
        return self.sems[name]

    def v(self, name):
        return self.cnt.get(name, 0)

    def bump(self, name, n):
        self.cnt[name] = self.v(name) + n
        return self.cnt[name]

    serial = False
    last_inc = None

    def op(self, eng, meth, *a, w=(), i=(), **kw):
        w = list(w)
        i = list(i)
        if self.serial:
            if self.last_inc is not None:
                w.append(self.last_inc)
            if not i:
                i.append(("ser", 1))
            s0, n0 = i[0]
            self.last_inc = (s0, self.v(s0) + n0)
        for s, val in w:
            if s is not None and val > 0:
                self.max_wait[s] = max(self.max_wait.get(s, 0), val)
        if meth == "dma_start":
            for s, n in i:
                self.is_dma[s] = True
        tgts = [(self.sem(s), val) for s, val in w if s is not None and val > 0]
        incs = [(self.sem(s), n) for s, n in i]
        for s, n in i:
            self.bump(s, n)

        def f(e, meth=meth, a=a, kw=kw, tgts=tgts, incs=incs):
            for h, val in tgts:
                e.wait_ge(h, val)
            ins = getattr(e, meth)(*a, **kw)
            for h, n in incs:
                ins.then_inc(h, n)
        self.q[eng].append(f)

    def wait(self, eng, s, val):
        if val <= 0:
            return
        self.max_wait[s] = max(self.max_wait.get(s, 0), val)
        h = self.sem(s)
        self.q[eng].append(lambda e, h=h, v=val: e.wait_ge(h, v))

    def drain_dma_sems(self):
        for s in sorted(self.is_dma):
            if self.max_wait.get(s, 0) < self.v(s):
                self.wait("sync", s, self.v(s))


class Rot:

    def __init__(self, n):
        self.n = n
        self.free = [None] * n
        self.sticky = {}
        self.i = 0

    def next(self):
        slot = self.i % self.n
        self.i += 1
        rec = list(self.free[slot]) if self.free[slot] else [(None, 0)]
        if slot in self.sticky:
            rec.append(self.sticky.pop(slot))
        return slot, rec

    def set_free(self, slot, sem, val):
        self.free[slot] = [(sem, val)]

    def set_sticky(self, slot, sem, val):
        self.sticky[slot] = (sem, val)


def ceil_div(a, b):
    return (a + b - 1) // b


def ap3(base, elem_off, dims):
    part = [base.ap[0][0], base.ap[0][1]]
    return bass.AP(base.tensor, base.offset + elem_off, [part] + [list(d) for d in dims])


def build_program(taps=(), do=("weff", "ffn1", "attn", "ffn2"), attn_seqs=2, attn_groups=8, attn_blocks=4, attn_oproj=True, attn_stage=9, ffn_chunks=4, ffn_stage=9, ln_serial=0):
    nc = bass.Bass("TRN2", target_bir_lowering=False)
    ap = lambda t: t.ap()

    xT = ap(nc.dram_tensor("xT", [NSEQ, D, SEQ], F32, kind="ExternalInput"))
    gu_pre = ap(nc.dram_tensor("gu_pre", [E, 22, 128, KT * 128], F32, kind="ExternalInput"))
    dn_pre = ap(nc.dram_tensor("dn_pre", [E, 16, 128, 11 * 128], F32, kind="ExternalInput"))
    shg_pre = ap(nc.dram_tensor("shg_pre", [44, 128, KT * 128], F32, kind="ExternalInput"))
    shu_pre = ap(nc.dram_tensor("shu_pre", [44, 128, KT * 128], F32, kind="ExternalInput"))
    shd_pre = ap(nc.dram_tensor("shd_pre", [16, 128, 44 * 128], F32, kind="ExternalInput"))
    wq_pre = ap(nc.dram_tensor("wq_pre", [16, 128, KT * 128], F32, kind="ExternalInput"))
    wk_pre = ap(nc.dram_tensor("wk_pre", [8, 128, KT * 128], F32, kind="ExternalInput"))
    wv_pre = ap(nc.dram_tensor("wv_pre", [8, 128, KT * 128], F32, kind="ExternalInput"))
    wo_pre = ap(nc.dram_tensor("wo_pre", [16, 128, KT * 128], F32, kind="ExternalInput"))
    a_d = {k: ap(nc.dram_tensor(f"a_{k}", [R, D], F32, kind="ExternalInput")) for k in "qkvo"}
    bT_d = {k: ap(nc.dram_tensor(f"bT_{k}", [R, D if k in "qo" else D // 2], F32,
                                 kind="ExternalInput")) for k in "qkvo"}
    srow_d = ap(nc.dram_tensor("srow", [1, 8], F32, kind="ExternalInput"))
    grow_d = ap(nc.dram_tensor("grow", [1, NH], F32, kind="ExternalInput"))
    rsT_d = ap(nc.dram_tensor("rsT", [D, 9], F32, kind="ExternalInput"))
    identb_d = ap(nc.dram_tensor("identb", [128, 128], BF16, kind="ExternalInput"))
    ident9_d = ap(nc.dram_tensor("ident9", [9, 9], F32, kind="ExternalInput"))
    trig_d = ap(nc.dram_tensor("trig", [64, 2 * SEQ], BF16, kind="ExternalInput"))
    mask_d = ap(nc.dram_tensor("maskadd", [128, 128], F32, kind="ExternalInput"))
    selrope_d = ap(nc.dram_tensor("selrope", [128, 64], BF16, kind="ExternalInput"))
    maskTb_d = ap(nc.dram_tensor("maskTb", [128, 128], BF16, kind="ExternalInput"))

    uT = ap(nc.dram_tensor("uT", [NSEQ, D, SEQ], F32, kind="Internal"))
    vT = ap(nc.dram_tensor("vT", [NSEQ, D, SEQ], F32, kind="Internal"))
    weff = ap(nc.dram_tensor("weff", [48, 128, KT * 128], BF16, kind="Internal"))
    wbf_scr = ap(nc.dram_tensor("wbf_scr", [456, 128, KT * 128], BF16, kind="Internal"))
    yD = ap(nc.dram_tensor("yD", [NSEQ, 16, 128, SEQ], BF16, kind="Internal"))
    oT = ap(nc.dram_tensor("oT", [NSEQ, D, SEQ], F32, kind="ExternalOutput"))
    qk_dbg = None
    y_dbg = None
    h_dbg = None
    w_dbg = None
    f_dbg = None
    if "f0" in taps:
        f_dbg = ap(nc.dram_tensor("f_dbg", [16, 128, TC], F32, kind="ExternalOutput"))
    if "w0" in taps:
        w_dbg = ap(nc.dram_tensor("w_dbg", [128, 72], BF16, kind="ExternalOutput"))
    if "h0" in taps:
        h_dbg = ap(nc.dram_tensor("h_dbg", [KT, 128, TC], BF16, kind="ExternalOutput"))
    if "qk" in taps:
        qk_dbg = ap(nc.dram_tensor("qk_dbg", [4, 128, SEQ], BF16, kind="ExternalOutput"))
    if "yD" in taps:
        y_dbg = ap(nc.dram_tensor("y_dbg", [NSEQ, 16, 128, SEQ], BF16,
                                  kind="ExternalOutput"))
    tap_ts = {}
    for t in taps:
        if t in ("qk", "yD", "h0", "w0", "f0"):
            continue
        if t == "weff":
            tap_ts[t] = (ap(nc.dram_tensor("weff_dbg", [48, 128, KT * 128], BF16,
                                           kind="ExternalOutput")), weff)
        else:
            srct = {"uT": uT, "vT": vT}[t]
            tap_ts[t] = (ap(nc.dram_tensor(t + "_dbg", [NSEQ, D, SEQ], F32,
                                           kind="ExternalOutput")), srct)

    kb = KB(nc)

    def sb(name, shape, dtype):
        return nc.alloc_sbuf_tensor(name, shape, dtype)

    wst = [sb(f"wst{i}", [128, 2048], BF16) for i in range(4)]
    trig = sb("trig_s", [64, 2 * SEQ], BF16)
    skin = sb("skin", [1, 3072], F32)
    xf = [sb(f"xf{i}", [128, TC], F32) for i in range(2)]
    xb = [sb(f"xb{i}", [128, TC], BF16) for i in range(2)]
    sqb = [sb(f"sqb{i}", [128, TC], BF16) for i in range(2)]
    tn = sb("tn", [128, TC], F32)
    identb = sb("identb_s", [128, 128], BF16)
    ident9 = sb("ident9_s", [9, 9], F32)
    ones_rb = sb("ones_rb", [1, 128], BF16)
    ones_rf = sb("ones_rf", [1, 128], F32)
    ones_cb = sb("ones_cb", [128, 1], BF16)
    maskadd = sb("maskadd_s", [128, 128], F32)
    rs_bf = sb("rs_bf", [128, KT * 9], BF16)
    grow = sb("grow_s", [1, NH], F32)
    srow = sb("srow_s", [1, 8], F32)
    negc = sb("negc_s", [128, NH], F32)
    scol = sb("scol_s", [128, 8], F32)
    b30 = sb("b30", [128, 1], F32)
    beps_ln = sb("beps_ln", [128, 1], F32)
    beps_rms = sb("beps_rms", [128, 1], F32)
    wsig = sb("wsig", [128, 72], F32)
    wsig_bf = sb("wsig_bf", [128, 72], BF16)
    wrow_tmp = sb("wrow_tmp", [1, TC], BF16)
    lg9 = sb("lg9", [9, 128], F32)
    lgt = sb("lgt", [128, 9], F32)
    topt = sb("topt", [128, 9], F32)
    m1c = sb("m1c", [128, 1], F32)
    m2c = sb("m2c", [128, 1], F32)
    nm1 = sb("nm1", [128, 1], F32)
    dsum = sb("dsum", [128, 1], F32)
    dinv = sb("dinv", [128, 1], F32)
    lsum = [sb(f"lsum{i}", [128, 1], F32) for i in range(2)]
    linv = [sb(f"linv{i}", [128, 1], F32) for i in range(2)]
    diagb = [sb(f"diagb{i}", [128, 128], BF16) for i in range(2)]
    rsq = [sb(f"rsq{i}", [128, 512], BF16) for i in range(2)]
    ropA = sb("ropA", [128, SEQ], BF16)
    selrope = sb("selrope_s", [128, 64], BF16)
    maskTb = sb("maskTb_s", [128, 128], BF16)
    ystg = [sb(f"ystg{i}", [128, 512], BF16) for i in range(2)]

    BIGB = 132 * 1024
    big_h = nc.alloc_sbuf_tensor("big", [128, BIGB], I8)
    views = {F32: big_h.bitcast(F32), BF16: big_h.bitcast(BF16)}

    def at(byte_off, parts, cols, dtype):
        sz = 4 if dtype is F32 else 2
        assert byte_off % 64 == 0 and byte_off + cols * sz <= BIGB
        return views[dtype][0:parts, byte_off // sz: byte_off // sz + cols]

    KI = 1024
    f_sb = [at(ms * 4 * TC, 128, TC, F32) for ms in range(16)]
    hsb = [at(64 * KI + kt * 2 * TC, 128, TC, BF16) for kt in range(KT)]
    act = [at(96 * KI + s * 2 * TC, 128, TC, BF16) for s in range(11)]
    silu_b = [at(118 * KI + i * 2 * TC, 128, TC, BF16) for i in range(2)]
    wb_sb = at(122 * KI, 128, TC, BF16)
    off_b = [at(124 * KI + i * 2048, 128, 512, F32) for i in range(2)]
    rff = [at(128 * KI + i * 2048, 128, 512, F32) for i in range(2)]
    wtmp = [at(0 + i * 8192, 128, 2048, F32) for i in range(2)]
    _ab_off = 16 * 1024
    a_bf = {}
    bT_bf = {}
    for _i, _k in enumerate("qkvo"):
        a_bf[_k] = at(_ab_off, R, D, BF16)
        _ab_off += 2 * D
        _w = D if _k in "qo" else D // 2
        bT_bf[_k] = at(_ab_off, R, _w, BF16)
        _ab_off += 2 * _w
    h_full = [at(kt * 2 * SEQ, 128, SEQ, BF16) for kt in range(KT)]
    qg = [at(64 * KI + j * 2 * SEQ, 128, SEQ, BF16) for j in range(2)]
    kg = at(72 * KI, 128, SEQ, BF16)
    vg = at(76 * KI, 128, SEQ, BF16)
    v_tm = at(80 * KI, 128, KT * 128, BF16)
    p_sb = [at(84 * KI, 128, SEQ, BF16), at(104 * KI, 128, SEQ, BF16)]
    stag = at(88 * KI, 128, 16 * 512, BF16)
    yn = [at(64 * KI + kt * 1024, 128, 512, BF16) for kt in range(KT)]
    ofo = [at(80 * KI + i * 2048, 128, 512, F32) for i in range(2)]
    rfo = [at(84 * KI + i * 2048, 128, 512, F32) for i in range(2)]

    psA = nc.alloc_psum_tensor("psA", [128, 2048], F32).ap()
    psB = nc.alloc_psum_tensor("psB", [128, 2048], F32).ap()

    r_wst = Rot(4)
    r_xf = Rot(2)
    r_xb = Rot(2)
    r_off = Rot(2)
    r_rff = Rot(2)
    r_ystg = Rot(2)
    r_yn = Rot(1)
    r_ofo = Rot(2)
    r_rfo = Rot(2)
    r_silu = Rot(2)
    st = {"wl_sem": "wl1", "wc_sem": "wc1"}

    def wload(src_ap, cast=True):
        slot, frees = r_wst.next()
        sem = f"wl{'c' if cast else 'p'}{slot}"
        kb.op("gpsimd" if cast else "sync", "dma_start",
              out=wst[slot][:, 0:src_ap.shape[-1]], in_=src_ap,
              w=tuple(frees), i=((sem, 16),))
        return slot, sem, kb.v(sem)

    def wload_ffn(fp32_ap, sid, first):
        width = fp32_ap.shape[-1]
        if first:
            slot, sem, v = wload(fp32_ap, cast=True)
            kb.op("sync", "dma_start", out=wbf_scr[sid][:, 0:width],
                  in_=wst[slot][:, 0:width], w=((sem, v),),
                  i=((f"scb{slot}", 16),))
            r_wst.set_sticky(slot, f"scb{slot}", kb.v(f"scb{slot}"))
            st.setdefault("scb_rec", {})[sid] = (f"scb{slot}", kb.v(f"scb{slot}"))
            return slot, sem, v
        slot, frees = r_wst.next()
        sem = f"wlp{slot}"
        w = list(frees) + [st["scb_rec"][sid]]
        kb.op("sync", "dma_start", out=wst[slot][:, 0:width],
              in_=wbf_scr[sid][:, 0:width], w=tuple(w), i=((sem, 16),))
        return slot, sem, kb.v(sem)

    def emit_setup():
        for dst, src in [(identb, identb_d), (ident9, ident9_d), (trig, trig_d),
                         (maskadd, mask_d), (grow, grow_d), (srow, srow_d),
                         (selrope, selrope_d), (maskTb, maskTb_d)]:
            kb.op("sync", "dma_start", out=dst[:], in_=src, i=(("cl", 16),))
        rsr = rsT_d.rearrange("(kt p) e -> p kt e", p=128)
        kb.op("gpsimd", "dma_start",
              out=rs_bf[:].rearrange("p (kt e) -> p kt e", e=9), in_=rsr,
              i=(("clg", 16),))
        for kk in "qkvo":
            kb.op("gpsimd", "dma_start", out=a_bf[kk][:], in_=a_d[kk], i=(("clg", 16),))
            kb.op("gpsimd", "dma_start", out=bT_bf[kk][:], in_=bT_d[kk], i=(("clg", 16),))
        nl = kb.v("cl")
        kb.op("vector", "memset", ones_rb[:], 1.0)
        kb.op("vector", "memset", ones_rf[:], 1.0)
        kb.op("vector", "memset", ones_cb[:], 1.0)
        kb.op("vector", "memset", b30[:], 30.0)
        kb.op("vector", "memset", beps_ln[:], EPS_LN)
        kb.op("vector", "memset", beps_rms[:], EPS_RMS, i=(("cv", 1),))
        kb.op("scalar", "activation", skin[0:1, 0:NH], grow[0:1, :], AF.Abs,
              w=(("cl", nl),), i=(("cs", 1),))
        kb.op("vector", "tensor_scalar", skin[0:1, 64:64 + NH], skin[0:1, 0:NH],
              -math.sqrt(HD), 30.0, ALU.mult, ALU.add,
              w=(("cs", kb.v("cs")), ("cv", kb.v("cv"))), i=(("cv", 1),))
        kb.op("tensor", "matmul", psA[:, 0:NH], ones_rf[:], skin[0:1, 64:64 + NH],
              start=True, stop=True, w=(("cv", kb.v("cv")),), i=(("ct", 1),))
        kb.op("tensor", "matmul", psA[:, 64:72], ones_rf[:], srow[0:1, :],
              start=True, stop=True, i=(("ct", 1),))
        kb.op("vector", "tensor_copy", negc[:], psA[:, 0:NH], w=(("ct", 1),))
        kb.op("vector", "tensor_copy", scol[:], psA[:, 64:72], w=(("ct", 2),),
              i=(("vr", 1),))

    def emit_weff():
        specs = [("q", wq_pre, 16, 0, 0), ("k", wk_pre, 8, 16, 1),
                 ("v", wv_pre, 8, 24, 2), ("o", wo_pre, 16, 32, 3)]
        li = 0
        for kk, pre, ns, base, sidx in specs:
            for s in range(ns):
                tslot = li % 2
                w = [("wstt", li - 1)] if li - 1 > 0 else []
                kb.op("sync", "dma_start", out=wtmp[tslot][:], in_=pre[s],
                      w=tuple(w), i=((f"swt{tslot}", 16),))
                for kt in range(KT):
                    w2 = []
                    if kt == 0:
                        w2 = [("cl", kb.v("cl")), ("clg", kb.v("clg")), ("vr", kb.v("vr"))]
                        if li > 0:
                            w2.append(("wstt", li))
                    kb.op("tensor", "matmul", psA[:, kt * 128:(kt + 1) * 128],
                          a_bf[kk][:, kt * 128:(kt + 1) * 128],
                          bT_bf[kk][:, s * 128:(s + 1) * 128],
                          start=True, stop=True, w=tuple(w2),
                          i=(("bat", 1),) if kt == KT - 1 else ())
                wslot, frees = r_wst.next()
                kb.op("vector", "scalar_tensor_tensor", wst[wslot][:],
                      wtmp[tslot][:], scol[:, sidx:sidx + 1], psA[:, 0:2048],
                      ALU.mult, ALU.add,
                      w=tuple(list(frees) + [(f"swt{tslot}", kb.v(f"swt{tslot}")),
                              ("bat", kb.v("bat")), ("vr", kb.v("vr"))]),
                      i=(("wstt", 1),))
                kb.op("sync", "dma_start", out=weff[base + s], in_=wst[wslot][:],
                      w=(("wstt", kb.v("wstt")),), i=((f"sws{wslot}", 16),))
                r_wst.set_free(wslot, f"sws{wslot}", kb.v(f"sws{wslot}"))
                li += 1
        kb.op("vector", "engine_nop", w=(("wstt", kb.v("wstt")),), i=(("vchunk", 1),))

    def emit_ln(src_seq, t0, dst_fn):
        mask = getattr(kb, "ln_mask", 0)
        kb.serial = bool(mask & 1)
        for kt in range(KT):
            bslot, bfrees = r_xb.next()
            kb.op("gpsimd", "dma_start", out=xb[bslot][:],
                  in_=src_seq[kt * 128:(kt + 1) * 128, t0:t0 + TC],
                  w=tuple(bfrees), i=((f"sxb{bslot}", 16),))
            xblv = kb.v(f"sxb{bslot}")
            kb.op("vector", "tensor_mul", sqb[bslot][:], xb[bslot][:], xb[bslot][:],
                  w=tuple([(f"sxb{bslot}", xblv)] + list(bfrees)), i=(("sq", 1),))
            for n in range(2):
                kb.op("tensor", "matmul", psA[0:1, n * 512:(n + 1) * 512],
                      ones_cb[:], xb[bslot][:, n * 512:(n + 1) * 512],
                      start=(kt == 0), stop=(kt == KT - 1),
                      w=((f"sxb{bslot}", xblv),) if n == 0 else ())
            for n in range(2):
                kb.op("tensor", "matmul", psA[0:1, 1024 + n * 512:1024 + (n + 1) * 512],
                      ones_cb[:], sqb[bslot][:, n * 512:(n + 1) * 512],
                      start=(kt == 0), stop=(kt == KT - 1),
                      w=(("sq", kb.v("sq")),) if n == 0 else (),
                      i=(("stk", 1),) if n == 1 else ())
            r_xb.set_free(bslot, "stk", kb.v("stk"))
        kb.serial = bool(mask & 2)
        sk_m = skin[0:1, 0:TC]
        sk_b = skin[0:1, 1024:1024 + TC]
        sk_c = skin[0:1, 2048:2048 + TC]
        kb.op("scalar", "activation", sk_m, psA[0:1, 0:TC], AF.Copy, scale=1.0 / D,
              w=(("stk", kb.v("stk")), ("vln", kb.v("vln")), ("prt", kb.v("prt")),
                 ("prv", kb.v("prv"))),
              i=(("sk", 1),))
        kb.op("scalar", "activation", sk_b, psA[0:1, 1024:1024 + TC], AF.Copy,
              scale=1.0 / D, w=(("sk", kb.v("sk")),), i=(("sk", 1),))
        kb.op("vector", "tensor_mul", sk_c, sk_m, sk_m, w=(("sk", kb.v("sk")),),
              i=(("vln", 1),))
        kb.op("vector", "tensor_sub", sk_b, sk_b, sk_c, w=(("vln", kb.v("vln")),),
              i=(("vln", 1),))
        kb.op("scalar", "activation", sk_c, sk_b, AF.Ln, bias=beps_ln[0:1, 0:1],
              w=(("vln", kb.v("vln")),), i=(("sk", 1),))
        kb.op("scalar", "activation", sk_c, sk_c, AF.Exp, scale=-0.5,
              w=(("sk", kb.v("sk")),), i=(("sk", 1),))
        kb.op("vector", "scalar_tensor_tensor", sk_m, sk_m, -1.0, sk_c,
              ALU.mult, ALU.mult, w=(("sk", kb.v("sk")),), i=(("vln", 1),))
        kb.serial = bool(mask & 4)
        for n in range(2):
            kb.op("tensor", "matmul", psB[:, n * 512:(n + 1) * 512], ones_rf[:],
                  sk_c[0:1, n * 512:(n + 1) * 512], start=True, stop=True,
                  w=(("vln", kb.v("vln")),) if n == 0 else ())
        for n in range(2):
            kb.op("tensor", "matmul", psB[:, 1024 + n * 512:1024 + (n + 1) * 512],
                  ones_rf[:], sk_m[0:1, n * 512:(n + 1) * 512], start=True, stop=True,
                  i=(("bc", 1),) if n == 1 else ())
        for kt in range(KT):
            slot, frees = r_xf.next()
            kb.op("sync", "dma_start", out=xf[slot][:],
                  in_=src_seq[kt * 128:(kt + 1) * 128, t0:t0 + TC],
                  w=tuple(frees), i=((f"sxf{slot}", 16),))
            kb.op("vector", "tensor_mul", tn[:], xf[slot][:], psB[:, 0:TC],
                  w=((f"sxf{slot}", kb.v(f"sxf{slot}")), ("bc", kb.v("bc")),
                     ("h", kb.v("h"))),
                  i=(("t1", 1),))
            r_xf.set_free(slot, "t1", kb.v("t1"))
            kb.op("vector", "tensor_add", dst_fn(kt), tn[:], psB[:, 1024:1024 + TC],
                  w=(("t1", kb.v("t1")),), i=(("h", 1),))

    def emit_router(h_base):
        for kt in range(KT):
            for n in range(2):
                w = []
                if kt == 0 and n == 0:
                    w = [("h", h_base + KT), ("sk", kb.v("sk"))]
                kb.op("tensor", "matmul", psA[0:9, n * 512:(n + 1) * 512],
                      rs_bf[:, kt * 9:(kt + 1) * 9],
                      hsb[kt][:, n * 512:(n + 1) * 512],
                      start=(kt == 0), stop=(kt == KT - 1), w=tuple(w),
                      i=(("rt", 1),) if kt == KT - 1 and n == 1 else ())
        for stt in range(8):
            wt = psB[:, (stt % 2) * 512:(stt % 2) * 512 + 9]
            w = [("rt", kb.v("rt")), ("wtp", kb.v("wtp"))]
            kb.op("vector", "tensor_copy", lg9[:],
                  psA[0:9, stt * 128:(stt + 1) * 128], w=tuple(w), i=(("rte", 1),))
            kb.op("tensor", "matmul", wt, lg9[:], ident9[:], start=True, stop=True,
                  w=(("rte", kb.v("rte")), ("wv", kb.v("wv") - 11),
                     ("rs2", kb.v("rs2") - 2)), i=(("wtp", 1),))
            b = kb.v("wv")
            kb.op("vector", "tensor_copy", lgt[:], wt, w=(("wtp", kb.v("wtp")),),
                  i=(("wv", 1),))
            kb.op("vector", "reduce_max", m1c[:], lgt[:, 0:8], AX.X,
                  w=(("wv", b + 1),), i=(("wv", 1),))
            kb.op("vector", "tensor_scalar", topt[:, 0:8], lgt[:, 0:8], m1c[:, 0:1],
                  -1e30, ALU.is_equal, ALU.mult, w=(("wv", b + 2),), i=(("wv", 1),))
            kb.op("vector", "tensor_add", topt[:, 0:8], topt[:, 0:8], lgt[:, 0:8],
                  w=(("wv", b + 3),), i=(("wv", 1),))
            kb.op("vector", "reduce_max", m2c[:], topt[:, 0:8], AX.X,
                  w=(("wv", b + 4),), i=(("wv", 1),))
            kb.op("vector", "tensor_scalar", nm1[:], m1c[:], -1.0, None, ALU.mult,
                  w=(("wv", b + 5),), i=(("wv", 1),))
            kb.op("scalar", "activation", topt[:, 0:8], lgt[:, 0:8], AF.Exp,
                  bias=nm1[:, 0:1], w=(("wv", b + 6),), i=(("rs2", 1),))
            kb.op("vector", "tensor_scalar", lgt[:, 0:8], lgt[:, 0:8], m2c[:, 0:1],
                  None, ALU.is_ge, w=(("rs2", kb.v("rs2")),), i=(("wv", 1),))
            kb.op("vector", "tensor_mul", topt[:, 0:8], topt[:, 0:8], lgt[:, 0:8],
                  w=(("wv", b + 7),), i=(("wv", 1),))
            kb.op("vector", "reduce_sum", dsum[:], topt[:, 0:8], AX.X,
                  w=(("wv", b + 8),), i=(("wv", 1),))
            kb.op("vector", "reciprocal", dinv[:], dsum[:], w=(("wv", b + 9),),
                  i=(("wv", 1),))
            kb.op("vector", "tensor_scalar", wsig[:, stt * 9:stt * 9 + 8], topt[:, 0:8],
                  dinv[:, 0:1], None, ALU.mult, w=(("wv", b + 10),), i=(("wv", 1),))
            kb.op("scalar", "activation", wsig[:, stt * 9 + 8:stt * 9 + 9], wt[:, 8:9],
                  AF.Sigmoid, w=(("wtp", kb.v("wtp")),), i=(("rs2", 1),))
        kb.op("gpsimd", "tensor_copy", wsig_bf[:], wsig[:],
              w=(("wv", kb.v("wv")), ("rs2", kb.v("rs2"))), i=(("wvb", 1),))

    DGRP = None

    def emit_unit(u, gate_aps, up_aps, down_ap_fn, wcol, fa_base, u0_extra, first_chunk):
        DGRP = [(psB, 0), (psB, 1024), (psA, 0), (psA, 1024)]
        w0 = [("wvb", kb.v("wvb"))]
        if u >= 1:
            w0.append(("fa", fa_base + (u - 1) * 16 + 14))
        else:
            w0 += u0_extra
        for stt in range(8):
            kb.op("tensor", "matmul", psB[0:1, stt * 128:(stt + 1) * 128],
                  wsig_bf[:, stt * 9 + wcol:stt * 9 + wcol + 1], identb[:],
                  start=(stt in (0, 4)), stop=(stt in (3, 7)),
                  w=tuple(w0) if stt == 0 else (), i=(("wrt", 1),) if stt == 7 else ())
        kb.op("vector", "tensor_copy", wrow_tmp[0:1, :], psB[0:1, 0:TC],
              w=(("wrt", kb.v("wrt")),), i=(("wre", 1),))
        for n in range(2):
            kb.op("tensor", "matmul", psB[:, n * 512:(n + 1) * 512], ones_rb[:],
                  wrow_tmp[0:1, n * 512:(n + 1) * 512], start=True, stop=True,
                  w=(("wre", kb.v("wre")),) if n == 0 else (),
                  i=(("wbp", 1),) if n == 1 else ())
        kb.op("vector", "tensor_copy", wb_sb[:], psB[:, 0:TC],
              w=(("wbp", kb.v("wbp")),), i=(("wbv", 1),))
        si_base = kb.v("si")
        mu_base = kb.v("mu")
        am_base = kb.v("am")
        for s in range(11):
            gslot, gsem_g, gv = wload_ffn(gate_aps[s], u * 38 + 2 * s, first_chunk)
            uslot, gsem_u, uv = wload_ffn(up_aps[s], u * 38 + 2 * s + 1, first_chunk)
            ps = psA if s % 2 == 0 else psB
            wv = [(gsem_g, gv), (gsem_u, uv)]
            if s >= 2:
                wv += [("si", si_base + s - 1), ("mu", mu_base + s - 1)]
            if s == 1:
                wv += [("wbv", kb.v("wbv")),
                       ("fa", fa_base + (u - 1) * 16 + 14) if u >= 1 else (None, 0)]
            if s == 0 and u >= 1:
                wv += [("fa", fa_base + (u - 1) * 16 + 16)]
            first = True
            for kt in range(KT):
                for half, slot in ((0, gslot), (1, uslot)):
                    for n in range(2):
                        kb.op("tensor", "matmul",
                              ps[:, half * TC + n * 512: half * TC + (n + 1) * 512],
                              wst[slot][:, kt * 128:(kt + 1) * 128],
                              hsb[kt][:, n * 512:(n + 1) * 512],
                              start=(kt == 0), stop=(kt == KT - 1),
                              w=tuple(wv) if first else (),
                              i=(("gu", 1),)
                              if kt == KT - 1 and half == 1 and n == 1 else ())
                        first = False
            r_wst.set_free(gslot, "gu", kb.v("gu"))
            r_wst.set_free(uslot, "gu", kb.v("gu"))
            sslot, sfrees = r_silu.next()
            kb.op("scalar", "activation", silu_b[sslot][:], ps[:, 0:TC], AF.Silu,
                  w=tuple([("gu", kb.v("gu"))] + list(sfrees)), i=(("si", 1),))
            kb.op("vector", "tensor_mul", act[s][:], silu_b[sslot][:], ps[:, TC:2 * TC],
                  w=(("si", kb.v("si")),), i=(("mu", 1),))
            r_silu.set_free(sslot, "mu", kb.v("mu"))
            kb.op("vector", "tensor_mul", act[s][:], act[s][:], wb_sb[:],
                  w=(("mu", kb.v("mu")),), i=(("am", 1),))
        for ms in range(16):
            dslot, dsem, dv = wload_ffn(down_ap_fn(ms), u * 38 + 22 + ms, first_chunk)
            ps, c0 = DGRP[ms % 4]
            wv = [(dsem, dv)]
            if ms == 0:
                wv += [("am", am_base + 9), ("si", si_base + 10)]
            elif ms == 1:
                wv.append(("mu", mu_base + 10))
            elif ms == 2:
                wv.append(("si", si_base + 11))
            elif ms == 3:
                wv.append(("mu", mu_base + 11))
            else:
                wv.append(("fa", kb.v("fa") - 3))
            first = True
            for kt in range(11):
                if first:
                    kw = list(wv)
                elif ms == 0 and kt in (9, 10):
                    kw = [("am", am_base + kt + 1)]
                elif ms == 1 and kt == 10:
                    kw = [("am", am_base + 11)]
                else:
                    kw = []
                for n in range(2):
                    kb.op("tensor", "matmul", ps[:, c0 + n * 512:c0 + (n + 1) * 512],
                          wst[dslot][:, kt * 128:(kt + 1) * 128],
                          act[kt][:, n * 512:(n + 1) * 512],
                          start=(kt == 0), stop=(kt == 10),
                          w=tuple(kw) if n == 0 else (),
                          i=(("dn", 1),) if kt == 10 and n == 1 else ())
                    first = False
            r_wst.set_free(dslot, "dn", kb.v("dn"))
            if u == 0:
                kb.op("vector", "tensor_scalar", f_sb[ms][:], ps[:, c0:c0 + TC],
                      0.5, None, ALU.mult, w=(("dn", kb.v("dn")),), i=(("fa", 1),))
            else:
                kb.op("vector", "scalar_tensor_tensor", f_sb[ms][:],
                      ps[:, c0:c0 + TC], 0.5, f_sb[ms][:], ALU.mult, ALU.add,
                      w=(("dn", kb.v("dn")),), i=(("fa", 1),))

    def emit_ffn_chunk(ci, src, dst, sosem):
        _ffn_dst_marker = object()
        seq, half = divmod(ci, 2)
        t0 = half * TC
        h_base = kb.v("h")
        kb.wait("tensor", "fa", kb.v("fa"))
        kb.wait("tensor", "ove", kb.v("ove"))
        kb.wait("tensor", "wstt", kb.v("wstt"))
        kb.wait("vector", "f3", kb.v("f3"))
        kb.ln_mask = ln_serial
        emit_ln(src[seq], t0, lambda kt: hsb[kt][:])
        kb.ln_mask = 0
        kb.serial = False
        if h_dbg is not None and ci == 0 and dst is not _ffn_dst_marker:
            pass
        if h_dbg is not None and kb.v("hdumped") == 0:
            kb.wait("sync", "h", kb.v("h"))
            for kt in range(KT):
                kb.op("sync", "dma_start", out=h_dbg[kt], in_=hsb[kt][:],
                      i=(("tapst", 16),))
            kb.wait("sync", "tapst", kb.v("tapst"))
            kb.bump("hdumped", 1)
        if ffn_stage < 1:
            kb.op("vector", "engine_nop", w=(("h", kb.v("h")),), i=(("vchunk", 1),))
            return
        emit_router(h_base)
        fa_base = kb.v("fa")
        if w_dbg is not None and kb.v("wdumped") == 0:
            kb.wait("sync", "wvb", kb.v("wvb"))
            kb.op("sync", "dma_start", out=w_dbg, in_=wsig_bf[:], i=(("tapst", 16),))
            kb.wait("sync", "tapst", kb.v("tapst"))
            kb.bump("wdumped", 1)
        if ffn_stage < 2:
            kb.op("vector", "engine_nop", w=(("wv", kb.v("wv")),), i=(("vchunk", 1),))
            return
        u0_extra = [("rte", kb.v("rte")), ("t1", kb.v("t1"))]
        first = (kb.v("ffnchunks") == 0)
        kb.bump("ffnchunks", 1)
        for u in range(12):
            if u < 8:
                emit_unit(u, [gu_pre[u, s] for s in range(11)],
                          [gu_pre[u, 11 + s] for s in range(11)],
                          lambda ms, u=u: dn_pre[u, ms], u, fa_base, u0_extra, first)
            else:
                g = u - 8
                emit_unit(u, [shg_pre[g * 11 + s] for s in range(11)],
                          [shu_pre[g * 11 + s] for s in range(11)],
                          lambda ms, g=g: shd_pre[ms][:, g * 1408:(g + 1) * 1408],
                          8, fa_base, u0_extra, first)
        if f_dbg is not None and kb.v("fdumped") == 0:
            kb.wait("sync", "fa", kb.v("fa"))
            for ms in range(16):
                kb.op("sync", "dma_start", out=f_dbg[ms], in_=f_sb[ms][:],
                      i=(("tapst", 16),))
            kb.wait("sync", "tapst", kb.v("tapst"))
            kb.bump("fdumped", 1)
        for j in range(32):
            ms, hf = divmod(j, 2)
            rslot, rfrees = r_rff.next()
            kb.op("sync", "dma_start", out=rff[rslot][:],
                  in_=src[seq][ms * 128:(ms + 1) * 128, t0 + hf * 512:t0 + (hf + 1) * 512],
                  w=tuple(rfrees), i=((f"srf{rslot}", 16),))
            oslot, ofrees = r_off.next()
            kb.op("gpsimd", "tensor_add", off_b[oslot][:],
                  f_sb[ms][:, hf * 512:(hf + 1) * 512], rff[rslot][:],
                  w=tuple([(f"srf{rslot}", kb.v(f"srf{rslot}")),
                           ("fa", fa_base + 11 * 16 + ms + 1)] + list(ofrees)),
                  i=(("f3", 1),))
            r_rff.set_free(rslot, "f3", kb.v("f3"))
            ssem = f"{sosem}{oslot}"
            kb.op("scalar", "dma_start",
                  out=dst[seq][ms * 128:(ms + 1) * 128, t0 + hf * 512:t0 + (hf + 1) * 512],
                  in_=off_b[oslot][:], w=(("f3", kb.v("f3")),), i=((ssem, 16),))
            r_off.set_free(oslot, ssem, kb.v(ssem))
        kb.op("vector", "engine_nop", w=(("f3", kb.v("f3")),), i=(("vchunk", 1),))

    def emit_prep(tile, head, is_q):
        for b in range(4):
            slot = kb.v("rsqs") % 2
            w = [("rst", kb.v("rst") - 1), ("prv", kb.v("prv")),
                 ("pje", kb.v("pje"))]
            kb.op("scalar", "activation", rsq[slot][:], tile[:, b * 512:(b + 1) * 512],
                  AF.Square, w=tuple(w), i=(("rsqs", 1),))
            kb.op("tensor", "matmul", psB[0:1, b * 512:(b + 1) * 512], ones_cb[:],
                  rsq[slot][:], start=True, stop=True,
                  w=(("rsqs", kb.v("rsqs")), ("yv", kb.v("yv")), ("prv", kb.v("prv")),
                     ("rpe", kb.v("rpe")))
                  if b == 0 else (("rsqs", kb.v("rsqs")),),
                  i=(("rst", 1),))
        sk_a = skin[0:1, 0:SEQ]
        kb.op("scalar", "activation", sk_a, psB[0:1, 0:SEQ], AF.Ln,
              bias=beps_rms[0:1, 0:1], scale=1.0 / HD,
              w=(("rst", kb.v("rst")), ("prt", kb.v("prt")), ("bc", kb.v("bc")),
                 ("vln", kb.v("vln"))), i=(("prs", 1),))
        kb.op("scalar", "activation", sk_a, sk_a, AF.Exp, scale=-0.5,
              w=(("prs", kb.v("prs")),), i=(("prs", 1),))
        if is_q:
            kb.op("vector", "tensor_scalar", sk_a, sk_a, grow[0:1, head:head + 1],
                  None, ALU.mult, w=(("prs", kb.v("prs")),), i=(("prv", 1),))
            wsc = ("prv", kb.v("prv"))
        else:
            wsc = ("prs", kb.v("prs"))
        for n in range(4):
            kb.op("tensor", "matmul", psB[:, n * 512:(n + 1) * 512], ones_rf[:],
                  sk_a[0:1, n * 512:(n + 1) * 512], start=True, stop=True,
                  w=(wsc,) if n == 0 else (), i=(("prt", 1),) if n == 3 else ())
        kb.op("vector", "tensor_mul", tile[:], tile[:], psB[:, 0:SEQ],
              w=(("prt", kb.v("prt")),), i=(("prv", 1),))
        kb.op("gpsimd", "tensor_mul", ropA[0:64, :], tile[0:64, :], trig[:, 0:SEQ],
              w=(("prv", kb.v("prv")), ("rpe", kb.v("rpe"))))
        kb.op("gpsimd", "tensor_mul", ropA[64:128, :], tile[0:64, :],
              trig[:, SEQ:2 * SEQ], i=(("prp", 1),))
        for n in range(4):
            kb.op("tensor", "matmul", psB[0:64, n * 512:(n + 1) * 512],
                  selrope[:], ropA[:, n * 512:(n + 1) * 512],
                  start=True, stop=True,
                  w=(("prp", kb.v("prp")), ("prv", kb.v("prv"))) if n == 0 else (),
                  i=(("rpt", 1),) if n == 3 else ())
        kb.op("vector", "tensor_copy", tile[0:64, :], psB[0:64, 0:SEQ],
              w=(("rpt", kb.v("rpt")),), i=(("rpe", 1),))
        kb.op("gpsimd", "engine_nop", w=(("rpe", kb.v("rpe")),), i=(("prep", 1),))

    def emit_attn_seq(seq, src, dst, res):
        kb.wait("tensor", "fa", kb.v("fa"))
        kb.wait("tensor", "ove", kb.v("ove"))
        kb.wait("vector", "f3", kb.v("f3"))
        for eng in ("vector", "gpsimd", "scalar"):
            kb.wait(eng, "sov0", kb.v("sov0"))
            kb.wait(eng, "sov1", kb.v("sov1"))
        for half in range(2):
            for eng in ("sync", "gpsimd"):
                kb.wait(eng, "sou0", kb.v("sou0"))
                kb.wait(eng, "sou1", kb.v("sou1"))
            emit_ln(src[seq], half * TC,
                    lambda kt, half=half: h_full[kt][:, half * TC:(half + 1) * TC])
        h_fin = kb.v("h")
        for g in range(attn_groups):
            if attn_stage < 1:
                continue
            for widx, dtile in [(2 * g, qg[0]), (2 * g + 1, qg[1]), (16 + g, kg),
                                (24 + g, vg)]:
                slot, wsem, wv = wload(weff[widx], cast=False)
                for n in range(4):
                    pj = kb.v("pjt")
                    ps = psA[:, (pj % 2) * 512:(pj % 2) * 512 + 512]
                    w = [(wsem, wv), ("h", h_fin), ("pje", pj - 1),
                         ("ex", kb.v("ex")), ("sk", kb.v("sk"))]
                    for kt in range(KT):
                        kb.op("tensor", "matmul", ps,
                              wst[slot][:, kt * 128:(kt + 1) * 128],
                              h_full[kt][:, n * 512:(n + 1) * 512],
                              start=(kt == 0), stop=(kt == KT - 1),
                              w=tuple(w) if kt == 0 else (),
                              i=(("pjt", 1),) if kt == KT - 1 else ())
                    kb.op("vector", "tensor_copy", dtile[:, n * 512:(n + 1) * 512],
                          ps, w=(("pjt", kb.v("pjt")),), i=(("pje", 1),))
                r_wst.set_free(slot, "pjt", kb.v("pjt"))
            if attn_stage < 2:
                continue
            emit_prep(qg[0], 2 * g, True)
            emit_prep(qg[1], 2 * g + 1, True)
            emit_prep(kg, 0, False)
            if qk_dbg is not None and seq == 0 and g == 0:
                kb.wait("sync", "prep", kb.v("prep"))
                for _i, _t in enumerate([qg[0], qg[1], kg, vg]):
                    kb.op("sync", "dma_start", out=qk_dbg[_i], in_=_t[:],
                          i=(("tapst", 16),))
                kb.wait("sync", "tapst", kb.v("tapst"))
            if attn_stage < 3:
                continue
            for grp in range(4):
                w = [("prep", kb.v("prep")), ("vte", kb.v("vte")),
                     ("prv", kb.v("prv")), ("pje", kb.v("pje"))]
                for j in range(4):
                    kt = grp * 4 + j
                    kb.op("tensor", "matmul", psB[:, 1024 + j * 128:1024 + (j + 1) * 128],
                          vg[:, kt * 128:(kt + 1) * 128], identb[:],
                          start=(j == 0), stop=(j == 3),
                          w=tuple(w) if j == 0 else (),
                          i=(("vtt", 1),) if j == 3 else ())
                kb.op("vector", "tensor_copy", v_tm[:, grp * 512:(grp + 1) * 512],
                      psB[:, 1024:1536], w=(("vtt", kb.v("vtt")),), i=(("vte", 1),))
            if attn_stage < 4:
                continue
            for hh in range(2):
                head = 2 * g + hh
                tile = qg[hh]
                NQT = 4 * attn_blocks

                def emit_scores(qt):
                    gqt = qt
                    krange = (gqt + 1) * 128
                    nmm = ceil_div(krange, 512)
                    w = [("prep", kb.v("prep")), ("ex", kb.v("ex")),
                         ("vte", kb.v("vte"))]
                    last_bank = (krange - 1) // 512
                    for kbm in range(nmm):
                        n0 = kbm * 512
                        n1 = min(krange, n0 + 512)
                        kb.op("tensor", "matmul", psA[:, n0:n1],
                              tile[:, gqt * 128:(gqt + 1) * 128], kg[:, n0:n1],
                              start=True, stop=(kbm != last_bank),
                              w=tuple(w) if kbm == 0 else ())
                    kb.op("tensor", "matmul", psA[:, gqt * 128:krange],
                          maskTb[:], identb[:], start=False, stop=True,
                          i=(("sct", 1),))

                def emit_exp(qt):
                    krange = (qt + 1) * 128
                    pp = qt % 2
                    kb.op("scalar", "activation", p_sb[pp][:, 0:krange],
                          psA[:, 0:krange], AF.Exp,
                          bias=negc[:, head:head + 1], scale=INV_SQRT_HD,
                          accum_out=lsum[pp][:],
                          w=(("sct", kb.v("sct")), ("dg", kb.v("dg") - 1),
                             ("ptt", kb.v("ptt"))),
                          i=(("ex", 1),))
                    kb.op("vector", "reciprocal", linv[pp][:], lsum[pp][:],
                          w=(("ex", kb.v("ex")),), i=(("dg", 1),))
                    kb.op("vector", "tensor_scalar", diagb[pp][:], identb[:],
                          linv[pp][:, 0:1], None, ALU.mult,
                          w=(("dg", kb.v("dg")),), i=(("dg", 1),))

                def emit_trans(qt):
                    gqt = qt
                    pp = qt % 2
                    dgt = kb.v("dg") if qt == NQT - 1 else kb.v("dg") - 2
                    done = 0
                    while done <= gqt:
                        cnt = min(4, gqt + 1 - done)
                        tb = 1536 * (kb.v("ptg") % 2)
                        kb.bump("ptg", 1)
                        w2 = [("dg", dgt), ("ptv", kb.v("ptv") - 1)]
                        for j in range(cnt):
                            kb2 = done + j
                            kb.op("tensor", "matmul",
                                  psB[:, tb + j * 128:tb + (j + 1) * 128],
                                  p_sb[pp][:, kb2 * 128:(kb2 + 1) * 128], diagb[pp][:],
                                  start=(j == 0), stop=(j == cnt - 1),
                                  w=tuple(w2) if j == 0 else (),
                                  i=(("ptt", 1),) if j == cnt - 1 else ())
                        dst_ap = ap3(stag, done * 512 + (qt % 4) * 128,
                                     [[512, cnt], [1, 128]])
                        src_ap = psB[:, tb:tb + cnt * 128].rearrange(
                            "p (a c) -> p a c", c=128)
                        kb.op("vector", "tensor_copy", dst_ap, src_ap,
                              w=(("ptt", kb.v("ptt")),), i=(("ptv", 1),))
                        done += cnt

                def emit_zero(B):
                    nkt = 4 * (B + 1)
                    w = [("av", kb.v("av"))]
                    for qt in range(4):
                        for kb2 in range(4 * B + qt + 1, nkt):
                            kb.op("gpsimd", "memset",
                                  stag[:, kb2 * 512 + qt * 128:kb2 * 512 + (qt + 1) * 128],
                                  0.0, w=tuple(w), i=(("pz", 1),))
                            w = []

                def emit_av(B):
                    nkt = 4 * (B + 1)
                    wv = [("ptv", kb.v("ptv")), ("pz", kb.v("pz")),
                          ("yv", kb.v("yv")), ("vte", kb.v("vte"))]
                    for kb2 in range(nkt):
                        kb.op("tensor", "matmul", psB[:, 512:1024],
                              v_tm[:, kb2 * 128:(kb2 + 1) * 128],
                              stag[:, kb2 * 512:(kb2 + 1) * 512],
                              start=(kb2 == 0), stop=(kb2 == nkt - 1),
                              w=tuple(wv) if kb2 == 0 else (),
                              i=(("av", 1),) if kb2 == nkt - 1 else ())
                    yslot, yfrees = r_ystg.next()
                    kb.op("vector", "tensor_copy", ystg[yslot][:], psB[:, 512:1024],
                          w=tuple([("av", kb.v("av"))] + list(yfrees)), i=(("yv", 1),))
                    kb.op("scalar", "dma_start",
                          out=yD[seq, head, :, B * 512:(B + 1) * 512],
                          in_=ystg[yslot][:], w=(("yv", kb.v("yv")),),
                          i=((f"sys{yslot}", 16),))
                    r_ystg.set_free(yslot, f"sys{yslot}", kb.v(f"sys{yslot}"))

                emit_zero(0)
                emit_scores(0)
                emit_exp(0)
                for qt in range(NQT):
                    if qt % 4 == 0 and qt > 0:
                        emit_zero(qt // 4)
                    if qt + 1 < NQT:
                        emit_scores(qt + 1)
                        emit_exp(qt + 1)
                    emit_trans(qt)
                    if qt % 4 == 3:
                        emit_av(qt // 4)
        ys_fins = [("sys0", kb.v("sys0")), ("sys1", kb.v("sys1"))]
        for n in range(4 if attn_oproj else 0):
            _, yfrees = r_yn.next()
            for kt in range(KT):
                w = list(ys_fins) + list(yfrees) if kt == 0 else []
                kb.op("sync", "dma_start", out=yn[kt][:],
                      in_=yD[seq, kt, :, n * 512:(n + 1) * 512],
                      w=tuple(w), i=(("ylo", 16),))
            ylo_t = kb.v("ylo")
            for ms in range(16):
                slot, wsem, wv = wload(weff[32 + ms], cast=False)
                pj = kb.v("ott")
                ps = psA[:, (pj % 2) * 512:(pj % 2) * 512 + 512]
                w = [(wsem, wv), ("ylo", ylo_t), ("ove", pj - 1),
                     ("ex", kb.v("ex")), ("pje", kb.v("pje"))]
                for kt in range(KT):
                    kb.op("tensor", "matmul", ps,
                          wst[slot][:, kt * 128:(kt + 1) * 128], yn[kt][:],
                          start=(kt == 0), stop=(kt == KT - 1),
                          w=tuple(w) if kt == 0 else (),
                          i=(("ott", 1),) if kt == KT - 1 else ())
                r_wst.set_free(slot, "ott", kb.v("ott"))
                rslot, rfrees = r_rfo.next()
                kb.op("sync", "dma_start", out=rfo[rslot][:],
                      in_=res[seq][ms * 128:(ms + 1) * 128, n * 512:(n + 1) * 512],
                      w=tuple(rfrees), i=((f"sro{rslot}", 16),))
                oslot, ofrees = r_ofo.next()
                kb.op("vector", "scalar_tensor_tensor", ofo[oslot][:], ps, 1.0,
                      rfo[rslot][:], ALU.bypass, ALU.add,
                      w=tuple([("ott", kb.v("ott")),
                               (f"sro{rslot}", kb.v(f"sro{rslot}"))] + list(ofrees)),
                      i=(("ove", 1),))
                r_rfo.set_free(rslot, "ove", kb.v("ove"))
                kb.op("scalar", "dma_start",
                      out=dst[seq][ms * 128:(ms + 1) * 128, n * 512:(n + 1) * 512],
                      in_=ofo[oslot][:], w=(("ove", kb.v("ove")),),
                      i=((f"sov{oslot}", 16),))
                r_ofo.set_free(oslot, f"sov{oslot}", kb.v(f"sov{oslot}"))
            r_yn.set_free(0, "ott", kb.v("ott"))
        kb.op("vector", "engine_nop", w=(("ove", kb.v("ove")),), i=(("vchunk", 1),))

    emit_setup()
    if "weff" in do:
        emit_weff()
    st["wl_sem"], st["wc_sem"] = "wl1", "wc1"
    if "ffn1" in do:
        for ci in range(ffn_chunks):
            emit_ffn_chunk(ci, [xT[s] for s in range(NSEQ)],
                           [uT[s] for s in range(NSEQ)], "sou")
    if "attn" in do:
        for seq in range(attn_seqs):
            emit_attn_seq(seq, [uT[s] for s in range(NSEQ)],
                          [vT[s] for s in range(NSEQ)],
                          [uT[s] for s in range(NSEQ)])
    st["wl_sem"], st["wc_sem"] = "wl2", "wc2"
    if "ffn2" in do:
        for ci in range(4):
            for eng in ("sync", "gpsimd", "vector", "scalar"):
                kb.wait(eng, "sov0", kb.v("sov0"))
                kb.wait(eng, "sov1", kb.v("sov1"))
            emit_ffn_chunk(ci, [vT[s] for s in range(NSEQ)],
                           [oT[s] for s in range(NSEQ)], "soo")
    if y_dbg is not None:
        kb.wait("sync", "ys", kb.v("ys"))
        kb.op("sync", "dma_start", out=y_dbg, in_=yD, i=(("tapst", 16),))
        kb.wait("sync", "tapst", kb.v("tapst"))
    for name, (dstap, srcap) in tap_ts.items():
        gates = {"uT": [("sou0", kb.v("sou0")), ("sou1", kb.v("sou1"))],
                 "vT": [("sov0", kb.v("sov0")), ("sov1", kb.v("sov1"))],
                 "weff": [(f"sws{j}", kb.v(f"sws{j}")) for j in range(4)]}[name]
        kb.op("sync", "dma_start", out=dstap, in_=srcap, w=tuple(gates),
              i=(("tapst", 16),))
    fin = [("soo0", kb.v("soo0")), ("soo1", kb.v("soo1"))]
    if tap_ts:
        fin.append(("tapst", kb.v("tapst")))
    for s, val in fin:
        kb.wait("sync", s, val)
        kb.wait("tensor", s, val)

    kb.drain_dma_sems()
    with nc.Block() as block:
        block.sync(lambda e: [f(e) for f in kb.q["sync"]])
        block.tensor(lambda e: [f(e) for f in kb.q["tensor"]])
        block.vector(lambda e: [f(e) for f in kb.q["vector"]])
        block.scalar(lambda e: [f(e) for f in kb.q["scalar"]])
        block.gpsimd(lambda e: [f(e) for f in kb.q["gpsimd"]])
    return nc


def _pretile(w, S, KTn):
    return np.ascontiguousarray(
        w.reshape(S, 128, KTn, 128).transpose(0, 3, 2, 1).reshape(S, 128, KTn * 128))


def host_inputs(inputs):
    f32 = np.float32
    gup = np.stack([_pretile(np.asarray(inputs["L0_ffn_gate_up_proj"][e], f32), 22, 16)
                    for e in range(E)])
    dnp = np.stack([_pretile(np.asarray(inputs["L0_ffn_down_proj"][e], f32), 16, 11)
                    for e in range(E)])
    srow = np.concatenate([np.asarray(inputs[f"L0_attn_s_{k}"], f32).reshape(1)
                           for k in "qkvo"] + [np.zeros(4, f32)]).reshape(1, 8)
    rsT = np.ascontiguousarray(np.concatenate(
        [np.asarray(inputs["L0_ffn_router_weight"], f32).T,
         np.asarray(inputs["L0_ffn_shared_gate_w"], f32).T], axis=1))
    identb = np.eye(128, dtype=np.float32).astype(ml_dtypes.bfloat16)
    inv_freq = 1.0 / (10000.0 ** (np.arange(0, 64, 2, dtype=np.float64) / 64.0))
    fr = np.outer(inv_freq, np.arange(SEQ, dtype=np.float64))
    cos = np.cos(fr).astype(f32)
    sin = np.sin(fr).astype(f32)
    T1 = np.concatenate([cos, sin], axis=0)
    T2 = np.concatenate([sin, cos], axis=0)
    trig = np.concatenate([T1, T2], axis=1).astype(ml_dtypes.bfloat16)
    sel = np.zeros((128, 64), np.float32)
    for j in range(32):
        sel[j, j] = 1.0
        sel[j + 32, j] = 1.0
        sel[64 + j, 32 + j] = -1.0
        sel[64 + 32 + j, 32 + j] = 1.0
    common = dict(
        selrope=sel.astype(ml_dtypes.bfloat16),
        gu_pre=gup, dn_pre=dnp,
        shg_pre=_pretile(np.asarray(inputs["L0_ffn_sh_gate_w"], f32), 44, 16),
        shu_pre=_pretile(np.asarray(inputs["L0_ffn_sh_up_w"], f32), 44, 16),
        shd_pre=_pretile(np.asarray(inputs["L0_ffn_sh_down_w"], f32), 16, 44),
        wq_pre=_pretile(np.asarray(inputs["L0_attn_W_q"], f32), 16, 16),
        wk_pre=_pretile(np.asarray(inputs["L0_attn_W_k"], f32), 8, 16),
        wv_pre=_pretile(np.asarray(inputs["L0_attn_W_v"], f32), 8, 16),
        wo_pre=_pretile(np.asarray(inputs["L0_attn_W_o"], f32), 16, 16),
        a_q=np.asarray(inputs["L0_attn_A_q"], f32),
        a_k=np.asarray(inputs["L0_attn_A_k"], f32),
        a_v=np.asarray(inputs["L0_attn_A_v"], f32),
        a_o=np.asarray(inputs["L0_attn_A_o"], f32),
        bT_q=np.ascontiguousarray(np.asarray(inputs["L0_attn_B_q"], f32).T),
        bT_k=np.ascontiguousarray(np.asarray(inputs["L0_attn_B_k"], f32).T),
        bT_v=np.ascontiguousarray(np.asarray(inputs["L0_attn_B_v"], f32).T),
        bT_o=np.ascontiguousarray(np.asarray(inputs["L0_attn_B_o"], f32).T),
        srow=srow, grow=np.asarray(inputs["L0_attn_q_gain"], f32).reshape(1, NH),
        rsT=rsT, identb=identb, ident9=np.eye(9, dtype=f32), trig=trig,
        maskadd=np.ascontiguousarray(np.triu(np.full((128, 128), -1e30, f32), k=1)),
        maskTb=np.ascontiguousarray(
            np.tril(np.full((128, 128), -1e30, f32), k=-1)).astype(ml_dtypes.bfloat16),
    )
    x = np.asarray(inputs["L0_attn_x"], f32)
    xT = np.ascontiguousarray(x.transpose(0, 2, 1))
    return common, xT


def kernel(**inputs):
    common, xT = host_inputs(inputs)
    nc = build_program()
    in_maps = []
    for c in range(NCORE):
        m = dict(common)
        m["xT"] = xT[2 * c:2 * c + 2]
        in_maps.append(m)
    res = run_bass_kernel_spmd(nc, in_maps, core_ids=list(range(NCORE)))
    out = np.concatenate([r["oT"] for r in res.results], axis=0)
    return np.ascontiguousarray(out.transpose(0, 2, 1))


# version 05
# speedup vs baseline: 1.0017x; 1.0017x over previous
import math
import numpy as np
import ml_dtypes
import concourse.bass as bass
import concourse.mybir as mybir
from concourse.bass_utils import run_bass_kernel_spmd

F32 = mybir.dt.float32
BF16 = mybir.dt.bfloat16
I8 = mybir.dt.int8
AF = mybir.ActivationFunctionType
ALU = mybir.AluOpType
AX = mybir.AxisListType

D = 2048
SEQ = 2048
NSEQ = 2
NCORE = 8
TC = 1024
KT = D // 128
E = 8
NH = 16
HD = 128
R = 32
INV_SQRT_HD = 1.0 / math.sqrt(HD)
EPS_LN = 1e-5
EPS_RMS = 1.1920929e-07
ENGINES = ("sync", "tensor", "vector", "scalar", "gpsimd")


class KB:
    def __init__(self, nc):
        self.nc = nc
        self.q = {e: [] for e in ENGINES}
        self.sems = {}
        self.cnt = {}
        self.max_wait = {}
        self.is_dma = {}

    def sem(self, name):
        if name not in self.sems:
            self.sems[name] = self.nc.alloc_semaphore(name)
        return self.sems[name]

    def v(self, name):
        return self.cnt.get(name, 0)

    def bump(self, name, n):
        self.cnt[name] = self.v(name) + n
        return self.cnt[name]

    serial = False
    last_inc = None

    def op(self, eng, meth, *a, w=(), i=(), **kw):
        w = list(w)
        i = list(i)
        if self.serial:
            if self.last_inc is not None:
                w.append(self.last_inc)
            if not i:
                i.append(("ser", 1))
            s0, n0 = i[0]
            self.last_inc = (s0, self.v(s0) + n0)
        for s, val in w:
            if s is not None and val > 0:
                self.max_wait[s] = max(self.max_wait.get(s, 0), val)
        if meth == "dma_start":
            for s, n in i:
                self.is_dma[s] = True
        tgts = [(self.sem(s), val) for s, val in w if s is not None and val > 0]
        incs = [(self.sem(s), n) for s, n in i]
        for s, n in i:
            self.bump(s, n)

        def f(e, meth=meth, a=a, kw=kw, tgts=tgts, incs=incs):
            for h, val in tgts:
                e.wait_ge(h, val)
            ins = getattr(e, meth)(*a, **kw)
            for h, n in incs:
                ins.then_inc(h, n)
        self.q[eng].append(f)

    def wait(self, eng, s, val):
        if val <= 0:
            return
        self.max_wait[s] = max(self.max_wait.get(s, 0), val)
        h = self.sem(s)
        self.q[eng].append(lambda e, h=h, v=val: e.wait_ge(h, v))

    def drain_dma_sems(self):
        for s in sorted(self.is_dma):
            if self.max_wait.get(s, 0) < self.v(s):
                self.wait("sync", s, self.v(s))


class Rot:

    def __init__(self, n):
        self.n = n
        self.free = [None] * n
        self.sticky = {}
        self.i = 0

    def next(self):
        slot = self.i % self.n
        self.i += 1
        rec = list(self.free[slot]) if self.free[slot] else [(None, 0)]
        if slot in self.sticky:
            rec.append(self.sticky.pop(slot))
        return slot, rec

    def set_free(self, slot, sem, val):
        self.free[slot] = [(sem, val)]

    def set_sticky(self, slot, sem, val):
        self.sticky[slot] = (sem, val)


def ceil_div(a, b):
    return (a + b - 1) // b


def ap3(base, elem_off, dims):
    part = [base.ap[0][0], base.ap[0][1]]
    return bass.AP(base.tensor, base.offset + elem_off, [part] + [list(d) for d in dims])


def build_program(taps=(), do=("weff", "ffn1", "attn", "ffn2"), attn_seqs=2, attn_groups=8, attn_blocks=4, attn_oproj=True, attn_stage=9, ffn_chunks=4, ffn_stage=9, ln_serial=0):
    nc = bass.Bass("TRN2", target_bir_lowering=False)
    ap = lambda t: t.ap()

    xT = ap(nc.dram_tensor("xT", [NSEQ, D, SEQ], F32, kind="ExternalInput"))
    gu_pre = ap(nc.dram_tensor("gu_pre", [E, 22, 128, KT * 128], F32, kind="ExternalInput"))
    dn_pre = ap(nc.dram_tensor("dn_pre", [E, 16, 128, 11 * 128], F32, kind="ExternalInput"))
    shg_pre = ap(nc.dram_tensor("shg_pre", [44, 128, KT * 128], F32, kind="ExternalInput"))
    shu_pre = ap(nc.dram_tensor("shu_pre", [44, 128, KT * 128], F32, kind="ExternalInput"))
    shd_pre = ap(nc.dram_tensor("shd_pre", [16, 128, 44 * 128], F32, kind="ExternalInput"))
    wq_pre = ap(nc.dram_tensor("wq_pre", [16, 128, KT * 128], F32, kind="ExternalInput"))
    wk_pre = ap(nc.dram_tensor("wk_pre", [8, 128, KT * 128], F32, kind="ExternalInput"))
    wv_pre = ap(nc.dram_tensor("wv_pre", [8, 128, KT * 128], F32, kind="ExternalInput"))
    wo_pre = ap(nc.dram_tensor("wo_pre", [16, 128, KT * 128], F32, kind="ExternalInput"))
    a_d = {k: ap(nc.dram_tensor(f"a_{k}", [R, D], F32, kind="ExternalInput")) for k in "qkvo"}
    bT_d = {k: ap(nc.dram_tensor(f"bT_{k}", [R, D if k in "qo" else D // 2], F32,
                                 kind="ExternalInput")) for k in "qkvo"}
    srow_d = ap(nc.dram_tensor("srow", [1, 8], F32, kind="ExternalInput"))
    grow_d = ap(nc.dram_tensor("grow", [1, NH], F32, kind="ExternalInput"))
    rsT_d = ap(nc.dram_tensor("rsT", [D, 9], F32, kind="ExternalInput"))
    identb_d = ap(nc.dram_tensor("identb", [128, 128], BF16, kind="ExternalInput"))
    ident9_d = ap(nc.dram_tensor("ident9", [9, 9], F32, kind="ExternalInput"))
    trig_d = ap(nc.dram_tensor("trig", [64, 2 * SEQ], BF16, kind="ExternalInput"))
    mask_d = ap(nc.dram_tensor("maskadd", [128, 128], F32, kind="ExternalInput"))
    selrope_d = ap(nc.dram_tensor("selrope", [128, 64], BF16, kind="ExternalInput"))
    maskTb_d = ap(nc.dram_tensor("maskTb", [128, 128], BF16, kind="ExternalInput"))

    uT = ap(nc.dram_tensor("uT", [NSEQ, D, SEQ], F32, kind="Internal"))
    vT = ap(nc.dram_tensor("vT", [NSEQ, D, SEQ], F32, kind="Internal"))
    weff = ap(nc.dram_tensor("weff", [48, 128, KT * 128], BF16, kind="Internal"))
    wbf_scr = ap(nc.dram_tensor("wbf_scr", [456, 128, KT * 128], BF16, kind="Internal"))
    yD = ap(nc.dram_tensor("yD", [NSEQ, 16, 128, SEQ], BF16, kind="Internal"))
    oT = ap(nc.dram_tensor("oT", [NSEQ, D, SEQ], F32, kind="ExternalOutput"))
    qk_dbg = None
    y_dbg = None
    h_dbg = None
    w_dbg = None
    f_dbg = None
    if "f0" in taps:
        f_dbg = ap(nc.dram_tensor("f_dbg", [16, 128, TC], F32, kind="ExternalOutput"))
    if "w0" in taps:
        w_dbg = ap(nc.dram_tensor("w_dbg", [128, 72], BF16, kind="ExternalOutput"))
    if "h0" in taps:
        h_dbg = ap(nc.dram_tensor("h_dbg", [KT, 128, TC], BF16, kind="ExternalOutput"))
    if "qk" in taps:
        qk_dbg = ap(nc.dram_tensor("qk_dbg", [4, 128, SEQ], BF16, kind="ExternalOutput"))
    if "yD" in taps:
        y_dbg = ap(nc.dram_tensor("y_dbg", [NSEQ, 16, 128, SEQ], BF16,
                                  kind="ExternalOutput"))
    tap_ts = {}
    for t in taps:
        if t in ("qk", "yD", "h0", "w0", "f0"):
            continue
        if t == "weff":
            tap_ts[t] = (ap(nc.dram_tensor("weff_dbg", [48, 128, KT * 128], BF16,
                                           kind="ExternalOutput")), weff)
        else:
            srct = {"uT": uT, "vT": vT}[t]
            tap_ts[t] = (ap(nc.dram_tensor(t + "_dbg", [NSEQ, D, SEQ], F32,
                                           kind="ExternalOutput")), srct)

    kb = KB(nc)

    def sb(name, shape, dtype):
        return nc.alloc_sbuf_tensor(name, shape, dtype)

    wst = [sb(f"wst{i}", [128, 2048], BF16) for i in range(4)]
    trig = sb("trig_s", [64, 2 * SEQ], BF16)
    skin = sb("skin", [1, 3072], F32)
    xf = [sb(f"xf{i}", [128, TC], F32) for i in range(2)]
    xb = [sb(f"xb{i}", [128, TC], BF16) for i in range(2)]
    sqb = [sb(f"sqb{i}", [128, TC], BF16) for i in range(2)]
    tn = sb("tn", [128, TC], F32)
    identb = sb("identb_s", [128, 128], BF16)
    ident9 = sb("ident9_s", [9, 9], F32)
    ones_rb = sb("ones_rb", [1, 128], BF16)
    ones_rf = sb("ones_rf", [1, 128], F32)
    ones_cb = sb("ones_cb", [128, 1], BF16)
    maskadd = sb("maskadd_s", [128, 128], F32)
    rs_bf = sb("rs_bf", [128, KT * 9], BF16)
    grow = sb("grow_s", [1, NH], F32)
    srow = sb("srow_s", [1, 8], F32)
    negc = sb("negc_s", [128, NH], F32)
    scol = sb("scol_s", [128, 8], F32)
    b30 = sb("b30", [128, 1], F32)
    beps_ln = sb("beps_ln", [128, 1], F32)
    beps_rms = sb("beps_rms", [128, 1], F32)
    wsig = sb("wsig", [128, 72], F32)
    wsig_bf = sb("wsig_bf", [128, 72], BF16)
    wrow_tmp = sb("wrow_tmp", [1, TC], BF16)
    lg9 = sb("lg9", [9, 128], F32)
    lgt = sb("lgt", [128, 9], F32)
    topt = sb("topt", [128, 9], F32)
    m1c = sb("m1c", [128, 1], F32)
    m2c = sb("m2c", [128, 1], F32)
    nm1 = sb("nm1", [128, 1], F32)
    dsum = sb("dsum", [128, 1], F32)
    dinv = sb("dinv", [128, 1], F32)
    lsum = [sb(f"lsum{i}", [128, 1], F32) for i in range(2)]
    linv = [sb(f"linv{i}", [128, 1], F32) for i in range(2)]
    diagb = [sb(f"diagb{i}", [128, 128], BF16) for i in range(2)]
    rsq = [sb(f"rsq{i}", [128, 512], BF16) for i in range(2)]
    ropA = sb("ropA", [128, SEQ], BF16)
    selrope = sb("selrope_s", [128, 64], BF16)
    maskTb = sb("maskTb_s", [128, 128], BF16)
    ystg = [sb(f"ystg{i}", [128, 512], BF16) for i in range(2)]

    BIGB = 132 * 1024
    big_h = nc.alloc_sbuf_tensor("big", [128, BIGB], I8)
    views = {F32: big_h.bitcast(F32), BF16: big_h.bitcast(BF16)}

    def at(byte_off, parts, cols, dtype):
        sz = 4 if dtype is F32 else 2
        assert byte_off % 64 == 0 and byte_off + cols * sz <= BIGB
        return views[dtype][0:parts, byte_off // sz: byte_off // sz + cols]

    KI = 1024
    f_sb = [at(ms * 4 * TC, 128, TC, F32) for ms in range(16)]
    hsb = [at(64 * KI + kt * 2 * TC, 128, TC, BF16) for kt in range(KT)]
    act = [at(96 * KI + s * 2 * TC, 128, TC, BF16) for s in range(11)]
    silu_b = [at(118 * KI + i * 2 * TC, 128, TC, BF16) for i in range(2)]
    wb_sb = at(122 * KI, 128, TC, BF16)
    off_b = [at(124 * KI + i * 2048, 128, 512, F32) for i in range(2)]
    rff = [at(128 * KI + i * 2048, 128, 512, F32) for i in range(2)]
    wtmp = [at(0 + i * 8192, 128, 2048, F32) for i in range(2)]
    _ab_off = 16 * 1024
    a_bf = {}
    bT_bf = {}
    for _i, _k in enumerate("qkvo"):
        a_bf[_k] = at(_ab_off, R, D, BF16)
        _ab_off += 2 * D
        _w = D if _k in "qo" else D // 2
        bT_bf[_k] = at(_ab_off, R, _w, BF16)
        _ab_off += 2 * _w
    h_full = [at(kt * 2 * SEQ, 128, SEQ, BF16) for kt in range(KT)]
    qg = [at(64 * KI + j * 2 * SEQ, 128, SEQ, BF16) for j in range(2)]
    kg = at(72 * KI, 128, SEQ, BF16)
    vg = at(76 * KI, 128, SEQ, BF16)
    v_tm = at(80 * KI, 128, KT * 128, BF16)
    p_sb = [at(84 * KI, 128, SEQ, BF16), at(104 * KI, 128, SEQ, BF16)]
    stag = at(88 * KI, 128, 16 * 512, BF16)
    yn = [at(64 * KI + kt * 1024, 128, 512, BF16) for kt in range(KT)]
    ofo = [at(80 * KI + i * 2048, 128, 512, F32) for i in range(2)]
    rfo = [at(84 * KI + i * 2048, 128, 512, F32) for i in range(2)]

    psA = nc.alloc_psum_tensor("psA", [128, 2048], F32).ap()
    psB = nc.alloc_psum_tensor("psB", [128, 2048], F32).ap()

    r_wst = Rot(4)
    r_xf = Rot(2)
    r_xb = Rot(2)
    r_off = Rot(2)
    r_rff = Rot(2)
    r_ystg = Rot(2)
    r_yn = Rot(1)
    r_ofo = Rot(2)
    r_rfo = Rot(2)
    r_silu = Rot(2)
    st = {"wl_sem": "wl1", "wc_sem": "wc1"}

    def wload(src_ap, cast=True):
        slot, frees = r_wst.next()
        sem = f"wl{'c' if cast else 'p'}{slot}"
        kb.op("gpsimd" if cast else "sync", "dma_start",
              out=wst[slot][:, 0:src_ap.shape[-1]], in_=src_ap,
              w=tuple(frees), i=((sem, 16),))
        return slot, sem, kb.v(sem)

    def wload_ffn(fp32_ap, sid, first):
        width = fp32_ap.shape[-1]
        if first:
            slot, sem, v = wload(fp32_ap, cast=True)
            kb.op("sync", "dma_start", out=wbf_scr[sid][:, 0:width],
                  in_=wst[slot][:, 0:width], w=((sem, v),),
                  i=((f"scb{slot}", 16),))
            r_wst.set_sticky(slot, f"scb{slot}", kb.v(f"scb{slot}"))
            st.setdefault("scb_rec", {})[sid] = (f"scb{slot}", kb.v(f"scb{slot}"))
            return slot, sem, v
        slot, frees = r_wst.next()
        sem = f"wlp{slot}"
        w = list(frees) + [st["scb_rec"][sid]]
        kb.op("sync", "dma_start", out=wst[slot][:, 0:width],
              in_=wbf_scr[sid][:, 0:width], w=tuple(w), i=((sem, 16),))
        return slot, sem, kb.v(sem)

    def emit_setup():
        for dst, src in [(identb, identb_d), (ident9, ident9_d), (trig, trig_d),
                         (maskadd, mask_d), (grow, grow_d), (srow, srow_d),
                         (selrope, selrope_d), (maskTb, maskTb_d)]:
            kb.op("sync", "dma_start", out=dst[:], in_=src, i=(("cl", 16),))
        rsr = rsT_d.rearrange("(kt p) e -> p kt e", p=128)
        kb.op("gpsimd", "dma_start",
              out=rs_bf[:].rearrange("p (kt e) -> p kt e", e=9), in_=rsr,
              i=(("clg", 16),))
        for kk in "qkvo":
            kb.op("gpsimd", "dma_start", out=a_bf[kk][:], in_=a_d[kk], i=(("clg", 16),))
            kb.op("gpsimd", "dma_start", out=bT_bf[kk][:], in_=bT_d[kk], i=(("clg", 16),))
        nl = kb.v("cl")
        kb.op("vector", "memset", ones_rb[:], 1.0)
        kb.op("vector", "memset", ones_rf[:], 1.0)
        kb.op("vector", "memset", ones_cb[:], 1.0)
        kb.op("vector", "memset", b30[:], 30.0)
        kb.op("vector", "memset", beps_ln[:], EPS_LN)
        kb.op("vector", "memset", beps_rms[:], EPS_RMS, i=(("cv", 1),))
        kb.op("scalar", "activation", skin[0:1, 0:NH], grow[0:1, :], AF.Abs,
              w=(("cl", nl),), i=(("cs", 1),))
        kb.op("vector", "tensor_scalar", skin[0:1, 64:64 + NH], skin[0:1, 0:NH],
              -math.sqrt(HD), 30.0, ALU.mult, ALU.add,
              w=(("cs", kb.v("cs")), ("cv", kb.v("cv"))), i=(("cv", 1),))
        kb.op("tensor", "matmul", psA[:, 0:NH], ones_rf[:], skin[0:1, 64:64 + NH],
              start=True, stop=True, w=(("cv", kb.v("cv")),), i=(("ct", 1),))
        kb.op("tensor", "matmul", psA[:, 64:72], ones_rf[:], srow[0:1, :],
              start=True, stop=True, i=(("ct", 1),))
        kb.op("vector", "tensor_copy", negc[:], psA[:, 0:NH], w=(("ct", 1),))
        kb.op("vector", "tensor_copy", scol[:], psA[:, 64:72], w=(("ct", 2),),
              i=(("vr", 1),))

    def emit_weff():
        specs = [("q", wq_pre, 16, 0, 0), ("k", wk_pre, 8, 16, 1),
                 ("v", wv_pre, 8, 24, 2), ("o", wo_pre, 16, 32, 3)]
        li = 0
        for kk, pre, ns, base, sidx in specs:
            for s in range(ns):
                tslot = li % 2
                w = [("wstt", li - 1)] if li - 1 > 0 else []
                kb.op("sync", "dma_start", out=wtmp[tslot][:], in_=pre[s],
                      w=tuple(w), i=((f"swt{tslot}", 16),))
                for kt in range(KT):
                    w2 = []
                    if kt == 0:
                        w2 = [("cl", kb.v("cl")), ("clg", kb.v("clg")), ("vr", kb.v("vr"))]
                        if li > 0:
                            w2.append(("wstt", li))
                    kb.op("tensor", "matmul", psA[:, kt * 128:(kt + 1) * 128],
                          a_bf[kk][:, kt * 128:(kt + 1) * 128],
                          bT_bf[kk][:, s * 128:(s + 1) * 128],
                          start=True, stop=True, w=tuple(w2),
                          i=(("bat", 1),) if kt == KT - 1 else ())
                wslot, frees = r_wst.next()
                kb.op("vector", "scalar_tensor_tensor", wst[wslot][:],
                      wtmp[tslot][:], scol[:, sidx:sidx + 1], psA[:, 0:2048],
                      ALU.mult, ALU.add,
                      w=tuple(list(frees) + [(f"swt{tslot}", kb.v(f"swt{tslot}")),
                              ("bat", kb.v("bat")), ("vr", kb.v("vr"))]),
                      i=(("wstt", 1),))
                kb.op("sync", "dma_start", out=weff[base + s], in_=wst[wslot][:],
                      w=(("wstt", kb.v("wstt")),), i=((f"sws{wslot}", 16),))
                r_wst.set_free(wslot, f"sws{wslot}", kb.v(f"sws{wslot}"))
                li += 1
        kb.op("vector", "engine_nop", w=(("wstt", kb.v("wstt")),), i=(("vchunk", 1),))

    def emit_ln(src_seq, t0, dst_fn):
        mask = getattr(kb, "ln_mask", 0)
        kb.serial = bool(mask & 1)
        for kt in range(KT):
            bslot, bfrees = r_xb.next()
            kb.op("gpsimd", "dma_start", out=xb[bslot][:],
                  in_=src_seq[kt * 128:(kt + 1) * 128, t0:t0 + TC],
                  w=tuple(bfrees), i=((f"sxb{bslot}", 16),))
            xblv = kb.v(f"sxb{bslot}")
            kb.op("vector", "tensor_mul", sqb[bslot][:], xb[bslot][:], xb[bslot][:],
                  w=tuple([(f"sxb{bslot}", xblv)] + list(bfrees)), i=(("sq", 1),))
            for n in range(2):
                kb.op("tensor", "matmul", psA[0:1, n * 512:(n + 1) * 512],
                      ones_cb[:], xb[bslot][:, n * 512:(n + 1) * 512],
                      start=(kt == 0), stop=(kt == KT - 1),
                      w=((f"sxb{bslot}", xblv),) if n == 0 else ())
            for n in range(2):
                kb.op("tensor", "matmul", psA[0:1, 1024 + n * 512:1024 + (n + 1) * 512],
                      ones_cb[:], sqb[bslot][:, n * 512:(n + 1) * 512],
                      start=(kt == 0), stop=(kt == KT - 1),
                      w=(("sq", kb.v("sq")),) if n == 0 else (),
                      i=(("stk", 1),) if n == 1 else ())
            r_xb.set_free(bslot, "stk", kb.v("stk"))
        kb.serial = bool(mask & 2)
        sk_m = skin[0:1, 0:TC]
        sk_b = skin[0:1, 1024:1024 + TC]
        sk_c = skin[0:1, 2048:2048 + TC]
        kb.op("scalar", "activation", sk_m, psA[0:1, 0:TC], AF.Copy, scale=1.0 / D,
              w=(("stk", kb.v("stk")), ("vln", kb.v("vln")), ("prt", kb.v("prt")),
                 ("prv", kb.v("prv"))),
              i=(("sk", 1),))
        kb.op("scalar", "activation", sk_b, psA[0:1, 1024:1024 + TC], AF.Copy,
              scale=1.0 / D, w=(("sk", kb.v("sk")),), i=(("sk", 1),))
        kb.op("vector", "tensor_mul", sk_c, sk_m, sk_m, w=(("sk", kb.v("sk")),),
              i=(("vln", 1),))
        kb.op("vector", "tensor_sub", sk_b, sk_b, sk_c, w=(("vln", kb.v("vln")),),
              i=(("vln", 1),))
        kb.op("scalar", "activation", sk_c, sk_b, AF.Ln, bias=beps_ln[0:1, 0:1],
              w=(("vln", kb.v("vln")),), i=(("sk", 1),))
        kb.op("scalar", "activation", sk_c, sk_c, AF.Exp, scale=-0.5,
              w=(("sk", kb.v("sk")),), i=(("sk", 1),))
        kb.op("vector", "scalar_tensor_tensor", sk_m, sk_m, -1.0, sk_c,
              ALU.mult, ALU.mult, w=(("sk", kb.v("sk")),), i=(("vln", 1),))
        kb.serial = bool(mask & 4)
        for n in range(2):
            kb.op("tensor", "matmul", psB[:, n * 512:(n + 1) * 512], ones_rf[:],
                  sk_c[0:1, n * 512:(n + 1) * 512], start=True, stop=True,
                  w=(("vln", kb.v("vln")),) if n == 0 else ())
        for n in range(2):
            kb.op("tensor", "matmul", psB[:, 1024 + n * 512:1024 + (n + 1) * 512],
                  ones_rf[:], sk_m[0:1, n * 512:(n + 1) * 512], start=True, stop=True,
                  i=(("bc", 1),) if n == 1 else ())
        for kt in range(KT):
            slot, frees = r_xf.next()
            kb.op("sync", "dma_start", out=xf[slot][:],
                  in_=src_seq[kt * 128:(kt + 1) * 128, t0:t0 + TC],
                  w=tuple(frees), i=((f"sxf{slot}", 16),))
            kb.op("vector", "tensor_mul", tn[:], xf[slot][:], psB[:, 0:TC],
                  w=((f"sxf{slot}", kb.v(f"sxf{slot}")), ("bc", kb.v("bc")),
                     ("h", kb.v("h"))),
                  i=(("t1", 1),))
            r_xf.set_free(slot, "t1", kb.v("t1"))
            kb.op("vector", "tensor_add", dst_fn(kt), tn[:], psB[:, 1024:1024 + TC],
                  w=(("t1", kb.v("t1")),), i=(("h", 1),))

    def emit_router(h_base):
        for kt in range(KT):
            for n in range(2):
                w = []
                if kt == 0 and n == 0:
                    w = [("h", h_base + KT), ("sk", kb.v("sk"))]
                kb.op("tensor", "matmul", psA[0:9, n * 512:(n + 1) * 512],
                      rs_bf[:, kt * 9:(kt + 1) * 9],
                      hsb[kt][:, n * 512:(n + 1) * 512],
                      start=(kt == 0), stop=(kt == KT - 1), w=tuple(w),
                      i=(("rt", 1),) if kt == KT - 1 and n == 1 else ())
        for stt in range(8):
            wt = psB[:, (stt % 2) * 512:(stt % 2) * 512 + 9]
            w = [("rt", kb.v("rt")), ("wtp", kb.v("wtp"))]
            kb.op("vector", "tensor_copy", lg9[:],
                  psA[0:9, stt * 128:(stt + 1) * 128], w=tuple(w), i=(("rte", 1),))
            kb.op("tensor", "matmul", wt, lg9[:], ident9[:], start=True, stop=True,
                  w=(("rte", kb.v("rte")), ("wv", kb.v("wv") - 11),
                     ("rs2", kb.v("rs2") - 2)), i=(("wtp", 1),))
            b = kb.v("wv")
            kb.op("vector", "tensor_copy", lgt[:], wt, w=(("wtp", kb.v("wtp")),),
                  i=(("wv", 1),))
            kb.op("vector", "reduce_max", m1c[:], lgt[:, 0:8], AX.X,
                  w=(("wv", b + 1),), i=(("wv", 1),))
            kb.op("vector", "tensor_scalar", topt[:, 0:8], lgt[:, 0:8], m1c[:, 0:1],
                  -1e30, ALU.is_equal, ALU.mult, w=(("wv", b + 2),), i=(("wv", 1),))
            kb.op("vector", "tensor_add", topt[:, 0:8], topt[:, 0:8], lgt[:, 0:8],
                  w=(("wv", b + 3),), i=(("wv", 1),))
            kb.op("vector", "reduce_max", m2c[:], topt[:, 0:8], AX.X,
                  w=(("wv", b + 4),), i=(("wv", 1),))
            kb.op("vector", "tensor_scalar", nm1[:], m1c[:], -1.0, None, ALU.mult,
                  w=(("wv", b + 5),), i=(("wv", 1),))
            kb.op("scalar", "activation", topt[:, 0:8], lgt[:, 0:8], AF.Exp,
                  bias=nm1[:, 0:1], w=(("wv", b + 6),), i=(("rs2", 1),))
            kb.op("vector", "tensor_scalar", lgt[:, 0:8], lgt[:, 0:8], m2c[:, 0:1],
                  None, ALU.is_ge, w=(("rs2", kb.v("rs2")),), i=(("wv", 1),))
            kb.op("vector", "tensor_mul", topt[:, 0:8], topt[:, 0:8], lgt[:, 0:8],
                  w=(("wv", b + 7),), i=(("wv", 1),))
            kb.op("vector", "reduce_sum", dsum[:], topt[:, 0:8], AX.X,
                  w=(("wv", b + 8),), i=(("wv", 1),))
            kb.op("vector", "reciprocal", dinv[:], dsum[:], w=(("wv", b + 9),),
                  i=(("wv", 1),))
            kb.op("vector", "tensor_scalar", wsig[:, stt * 9:stt * 9 + 8], topt[:, 0:8],
                  dinv[:, 0:1], None, ALU.mult, w=(("wv", b + 10),), i=(("wv", 1),))
            kb.op("scalar", "activation", wsig[:, stt * 9 + 8:stt * 9 + 9], wt[:, 8:9],
                  AF.Sigmoid, w=(("wtp", kb.v("wtp")),), i=(("rs2", 1),))
        kb.op("gpsimd", "tensor_copy", wsig_bf[:], wsig[:],
              w=(("wv", kb.v("wv")), ("rs2", kb.v("rs2"))), i=(("wvb", 1),))

    DGRP = None

    def emit_unit(u, gate_aps, up_aps, down_ap_fn, wcol, fa_base, u0_extra, first_chunk):
        DGRP = [(psB, 0), (psB, 1024), (psA, 0), (psA, 1024)]
        w0 = [("wvb", kb.v("wvb"))]
        if u >= 1:
            w0.append(("fa", fa_base + (u - 1) * 16 + 14))
        else:
            w0 += u0_extra
        for stt in range(8):
            kb.op("tensor", "matmul", psB[0:1, stt * 128:(stt + 1) * 128],
                  wsig_bf[:, stt * 9 + wcol:stt * 9 + wcol + 1], identb[:],
                  start=(stt in (0, 4)), stop=(stt in (3, 7)),
                  w=tuple(w0) if stt == 0 else (), i=(("wrt", 1),) if stt == 7 else ())
        kb.op("vector", "tensor_copy", wrow_tmp[0:1, :], psB[0:1, 0:TC],
              w=(("wrt", kb.v("wrt")),), i=(("wre", 1),))
        for n in range(2):
            kb.op("tensor", "matmul", psB[:, n * 512:(n + 1) * 512], ones_rb[:],
                  wrow_tmp[0:1, n * 512:(n + 1) * 512], start=True, stop=True,
                  w=(("wre", kb.v("wre")),) if n == 0 else (),
                  i=(("wbp", 1),) if n == 1 else ())
        kb.op("vector", "tensor_copy", wb_sb[:], psB[:, 0:TC],
              w=(("wbp", kb.v("wbp")),), i=(("wbv", 1),))
        si_base = kb.v("si")
        mu_base = kb.v("mu")
        am_base = kb.v("am")
        for s in range(11):
            gslot, gsem_g, gv = wload_ffn(gate_aps[s], u * 38 + 2 * s, first_chunk)
            uslot, gsem_u, uv = wload_ffn(up_aps[s], u * 38 + 2 * s + 1, first_chunk)
            ps = psA if s % 2 == 0 else psB
            wv = [(gsem_g, gv), (gsem_u, uv)]
            if s >= 2:
                wv += [("si", si_base + s - 1), ("mu", mu_base + s - 1)]
            if s == 1:
                wv += [("wbv", kb.v("wbv")),
                       ("fa", fa_base + (u - 1) * 16 + 14) if u >= 1 else (None, 0)]
            if s == 0 and u >= 1:
                wv += [("fa", fa_base + (u - 1) * 16 + 16)]
            first = True
            for kt in range(KT):
                for half, slot in ((0, gslot), (1, uslot)):
                    for n in range(2):
                        kb.op("tensor", "matmul",
                              ps[:, half * TC + n * 512: half * TC + (n + 1) * 512],
                              wst[slot][:, kt * 128:(kt + 1) * 128],
                              hsb[kt][:, n * 512:(n + 1) * 512],
                              start=(kt == 0), stop=(kt == KT - 1),
                              w=tuple(wv) if first else (),
                              i=(("gu", 1),)
                              if kt == KT - 1 and half == 1 and n == 1 else ())
                        first = False
            r_wst.set_free(gslot, "gu", kb.v("gu"))
            r_wst.set_free(uslot, "gu", kb.v("gu"))
            sslot, sfrees = r_silu.next()
            kb.op("scalar", "activation", silu_b[sslot][:], ps[:, 0:TC], AF.Silu,
                  w=tuple([("gu", kb.v("gu"))] + list(sfrees)), i=(("si", 1),))
            kb.op("vector", "tensor_mul", act[s][:], silu_b[sslot][:], ps[:, TC:2 * TC],
                  w=(("si", kb.v("si")),), i=(("mu", 1),))
            r_silu.set_free(sslot, "mu", kb.v("mu"))
            kb.op("vector", "tensor_mul", act[s][:], act[s][:], wb_sb[:],
                  w=(("mu", kb.v("mu")),), i=(("am", 1),))
        for ms in range(16):
            dslot, dsem, dv = wload_ffn(down_ap_fn(ms), u * 38 + 22 + ms, first_chunk)
            ps, c0 = DGRP[ms % 4]
            wv = [(dsem, dv)]
            if ms == 0:
                wv += [("am", am_base + 9), ("si", si_base + 10)]
            elif ms == 1:
                wv.append(("mu", mu_base + 10))
            elif ms == 2:
                wv.append(("si", si_base + 11))
            elif ms == 3:
                wv.append(("mu", mu_base + 11))
            else:
                wv.append(("fa", kb.v("fa") - 3))
            first = True
            for kt in range(11):
                if first:
                    kw = list(wv)
                elif ms == 0 and kt in (9, 10):
                    kw = [("am", am_base + kt + 1)]
                elif ms == 1 and kt == 10:
                    kw = [("am", am_base + 11)]
                else:
                    kw = []
                for n in range(2):
                    kb.op("tensor", "matmul", ps[:, c0 + n * 512:c0 + (n + 1) * 512],
                          wst[dslot][:, kt * 128:(kt + 1) * 128],
                          act[kt][:, n * 512:(n + 1) * 512],
                          start=(kt == 0), stop=(kt == 10),
                          w=tuple(kw) if n == 0 else (),
                          i=(("dn", 1),) if kt == 10 and n == 1 else ())
                    first = False
            r_wst.set_free(dslot, "dn", kb.v("dn"))
            if u == 0:
                kb.op("vector", "tensor_scalar", f_sb[ms][:], ps[:, c0:c0 + TC],
                      0.5, None, ALU.mult, w=(("dn", kb.v("dn")),), i=(("fa", 1),))
            else:
                kb.op("vector", "scalar_tensor_tensor", f_sb[ms][:],
                      ps[:, c0:c0 + TC], 0.5, f_sb[ms][:], ALU.mult, ALU.add,
                      w=(("dn", kb.v("dn")),), i=(("fa", 1),))

    def emit_ffn_chunk(ci, src, dst, sosem):
        _ffn_dst_marker = object()
        seq, half = divmod(ci, 2)
        t0 = half * TC
        h_base = kb.v("h")
        kb.wait("tensor", "fa", kb.v("fa"))
        kb.wait("tensor", "ove", kb.v("ove"))
        kb.wait("tensor", "wstt", kb.v("wstt"))
        kb.ln_mask = ln_serial
        emit_ln(src[seq], t0, lambda kt: hsb[kt][:])
        kb.ln_mask = 0
        kb.serial = False
        if h_dbg is not None and ci == 0 and dst is not _ffn_dst_marker:
            pass
        if h_dbg is not None and kb.v("hdumped") == 0:
            kb.wait("sync", "h", kb.v("h"))
            for kt in range(KT):
                kb.op("sync", "dma_start", out=h_dbg[kt], in_=hsb[kt][:],
                      i=(("tapst", 16),))
            kb.wait("sync", "tapst", kb.v("tapst"))
            kb.bump("hdumped", 1)
        if ffn_stage < 1:
            kb.op("vector", "engine_nop", w=(("h", kb.v("h")),), i=(("vchunk", 1),))
            return
        emit_router(h_base)
        fa_base = kb.v("fa")
        if w_dbg is not None and kb.v("wdumped") == 0:
            kb.wait("sync", "wvb", kb.v("wvb"))
            kb.op("sync", "dma_start", out=w_dbg, in_=wsig_bf[:], i=(("tapst", 16),))
            kb.wait("sync", "tapst", kb.v("tapst"))
            kb.bump("wdumped", 1)
        if ffn_stage < 2:
            kb.op("vector", "engine_nop", w=(("wv", kb.v("wv")),), i=(("vchunk", 1),))
            return
        u0_extra = [("rte", kb.v("rte")), ("t1", kb.v("t1"))]
        first = (kb.v("ffnchunks") == 0)
        kb.bump("ffnchunks", 1)
        for u in range(12):
            if u < 8:
                emit_unit(u, [gu_pre[u, s] for s in range(11)],
                          [gu_pre[u, 11 + s] for s in range(11)],
                          lambda ms, u=u: dn_pre[u, ms], u, fa_base, u0_extra, first)
            else:
                g = u - 8
                emit_unit(u, [shg_pre[g * 11 + s] for s in range(11)],
                          [shu_pre[g * 11 + s] for s in range(11)],
                          lambda ms, g=g: shd_pre[ms][:, g * 1408:(g + 1) * 1408],
                          8, fa_base, u0_extra, first)
        if f_dbg is not None and kb.v("fdumped") == 0:
            kb.wait("sync", "fa", kb.v("fa"))
            for ms in range(16):
                kb.op("sync", "dma_start", out=f_dbg[ms], in_=f_sb[ms][:],
                      i=(("tapst", 16),))
            kb.wait("sync", "tapst", kb.v("tapst"))
            kb.bump("fdumped", 1)
        for j in range(32):
            ms, hf = divmod(j, 2)
            rslot, rfrees = r_rff.next()
            kb.op("sync", "dma_start", out=rff[rslot][:],
                  in_=src[seq][ms * 128:(ms + 1) * 128, t0 + hf * 512:t0 + (hf + 1) * 512],
                  w=tuple(rfrees), i=((f"srf{rslot}", 16),))
            oslot, ofrees = r_off.next()
            kb.op("vector", "tensor_add", off_b[oslot][:],
                  f_sb[ms][:, hf * 512:(hf + 1) * 512], rff[rslot][:],
                  w=tuple([(f"srf{rslot}", kb.v(f"srf{rslot}")),
                           ("fa", fa_base + 11 * 16 + ms + 1)] + list(ofrees)),
                  i=(("f3", 1),))
            r_rff.set_free(rslot, "f3", kb.v("f3"))
            ssem = f"{sosem}{oslot}"
            kb.op("scalar", "dma_start",
                  out=dst[seq][ms * 128:(ms + 1) * 128, t0 + hf * 512:t0 + (hf + 1) * 512],
                  in_=off_b[oslot][:], w=(("f3", kb.v("f3")),), i=((ssem, 16),))
            r_off.set_free(oslot, ssem, kb.v(ssem))
        kb.op("vector", "engine_nop", w=(("f3", kb.v("f3")),), i=(("vchunk", 1),))

    def emit_prep(tile, head, is_q):
        rpe_pb = kb.v("rpe")
        for b in range(4):
            slot = kb.v("rsqs") % 2
            w = [("rst", kb.v("rst") - 1), ("prv", kb.v("prv")),
                 ("pje", kb.v("pje"))]
            kb.op("scalar", "activation", rsq[slot][:], tile[:, b * 512:(b + 1) * 512],
                  AF.Square, w=tuple(w), i=(("rsqs", 1),))
            w2 = [("rsqs", kb.v("rsqs"))]
            if b == 0:
                w2 += [("yv", kb.v("yv"))]
            if rpe_pb:
                w2 += [("rpe", rpe_pb - 3 + b)]
            kb.op("tensor", "matmul", psB[0:1, b * 512:(b + 1) * 512], ones_cb[:],
                  rsq[slot][:], start=True, stop=True, w=tuple(w2), i=(("rst", 1),))
        rst_f = kb.v("rst")
        sk_a = skin[0:1, 0:SEQ]
        for n in range(4):
            blk = sk_a[0:1, n * 512:(n + 1) * 512]
            w = [("rst", rst_f - 3 + n)]
            if n == 0:
                w += [("prt", kb.v("prt")), ("bc", kb.v("bc")),
                      ("vln", kb.v("vln"))]
            kb.op("scalar", "activation", blk, psB[0:1, n * 512:(n + 1) * 512],
                  AF.Ln, bias=beps_rms[0:1, 0:1], scale=1.0 / HD,
                  w=tuple(w), i=(("prs", 1),))
            kb.op("scalar", "activation", blk, blk, AF.Exp, scale=-0.5,
                  w=(("prs", kb.v("prs")),), i=(("prs", 1),))
            if is_q:
                kb.op("vector", "tensor_scalar", blk, blk,
                      grow[0:1, head:head + 1], None, ALU.mult,
                      w=(("prs", kb.v("prs")),), i=(("prv", 1),))
                wsc = ("prv", kb.v("prv"))
            else:
                wsc = ("prs", kb.v("prs"))
            kb.op("tensor", "matmul", psB[:, n * 512:(n + 1) * 512], ones_rf[:],
                  blk, start=True, stop=True, w=(wsc,), i=(("prt", 1),))
            kb.op("vector", "tensor_mul", tile[:, n * 512:(n + 1) * 512],
                  tile[:, n * 512:(n + 1) * 512], psB[:, n * 512:(n + 1) * 512],
                  w=(("prt", kb.v("prt")),), i=(("prv", 1),))
            kb.op("gpsimd", "tensor_mul", ropA[0:64, n * 512:(n + 1) * 512],
                  tile[0:64, n * 512:(n + 1) * 512], trig[:, n * 512:(n + 1) * 512],
                  w=(("prv", kb.v("prv")), ("rpe", kb.v("rpe"))))
            kb.op("gpsimd", "tensor_mul", ropA[64:128, n * 512:(n + 1) * 512],
                  tile[0:64, n * 512:(n + 1) * 512],
                  trig[:, SEQ + n * 512:SEQ + (n + 1) * 512], i=(("prp", 1),))
            kb.op("tensor", "matmul", psB[0:64, n * 512:(n + 1) * 512],
                  selrope[:], ropA[:, n * 512:(n + 1) * 512],
                  start=True, stop=True, w=(("prp", kb.v("prp")),),
                  i=(("rpt", 1),))
            kb.op("vector", "tensor_copy", tile[0:64, n * 512:(n + 1) * 512],
                  psB[0:64, n * 512:(n + 1) * 512],
                  w=(("rpt", kb.v("rpt")),), i=(("rpe", 1),))
        kb.op("gpsimd", "engine_nop", w=(("rpe", kb.v("rpe")),), i=(("prep", 1),))

    def emit_attn_seq(seq, src, dst, res):
        kb.wait("tensor", "fa", kb.v("fa"))
        kb.wait("tensor", "ove", kb.v("ove"))
        for eng in ("vector", "gpsimd", "scalar"):
            kb.wait(eng, "sov0", kb.v("sov0"))
            kb.wait(eng, "sov1", kb.v("sov1"))
        for half in range(2):
            for eng in ("sync", "gpsimd"):
                kb.wait(eng, "sou0", kb.v("sou0"))
                kb.wait(eng, "sou1", kb.v("sou1"))
            emit_ln(src[seq], half * TC,
                    lambda kt, half=half: h_full[kt][:, half * TC:(half + 1) * TC])
        h_fin = kb.v("h")
        for g in range(attn_groups):
            if attn_stage < 1:
                continue
            for widx, dtile in [(2 * g, qg[0]), (2 * g + 1, qg[1]), (16 + g, kg),
                                (24 + g, vg)]:
                slot, wsem, wv = wload(weff[widx], cast=False)
                for n in range(4):
                    pj = kb.v("pjt")
                    ps = psA[:, (pj % 2) * 512:(pj % 2) * 512 + 512]
                    w = [(wsem, wv), ("h", h_fin), ("pje", pj - 1),
                         ("ex", kb.v("ex")), ("sk", kb.v("sk"))]
                    for kt in range(KT):
                        kb.op("tensor", "matmul", ps,
                              wst[slot][:, kt * 128:(kt + 1) * 128],
                              h_full[kt][:, n * 512:(n + 1) * 512],
                              start=(kt == 0), stop=(kt == KT - 1),
                              w=tuple(w) if kt == 0 else (),
                              i=(("pjt", 1),) if kt == KT - 1 else ())
                    kb.op("vector", "tensor_copy", dtile[:, n * 512:(n + 1) * 512],
                          ps, w=(("pjt", kb.v("pjt")),), i=(("pje", 1),))
                r_wst.set_free(slot, "pjt", kb.v("pjt"))
            if attn_stage < 2:
                continue
            emit_prep(qg[0], 2 * g, True)
            emit_prep(qg[1], 2 * g + 1, True)
            emit_prep(kg, 0, False)
            if qk_dbg is not None and seq == 0 and g == 0:
                kb.wait("sync", "prep", kb.v("prep"))
                for _i, _t in enumerate([qg[0], qg[1], kg, vg]):
                    kb.op("sync", "dma_start", out=qk_dbg[_i], in_=_t[:],
                          i=(("tapst", 16),))
                kb.wait("sync", "tapst", kb.v("tapst"))
            if attn_stage < 3:
                continue
            for grp in range(4):
                w = [("prep", kb.v("prep")), ("vte", kb.v("vte")),
                     ("prv", kb.v("prv")), ("pje", kb.v("pje"))]
                for j in range(4):
                    kt = grp * 4 + j
                    kb.op("tensor", "matmul", psB[:, 1024 + j * 128:1024 + (j + 1) * 128],
                          vg[:, kt * 128:(kt + 1) * 128], identb[:],
                          start=(j == 0), stop=(j == 3),
                          w=tuple(w) if j == 0 else (),
                          i=(("vtt", 1),) if j == 3 else ())
                kb.op("vector", "tensor_copy", v_tm[:, grp * 512:(grp + 1) * 512],
                      psB[:, 1024:1536], w=(("vtt", kb.v("vtt")),), i=(("vte", 1),))
            if attn_stage < 4:
                continue
            for hh in range(2):
                head = 2 * g + hh
                tile = qg[hh]
                NQT = 4 * attn_blocks

                def emit_scores(qt):
                    gqt = qt
                    krange = (gqt + 1) * 128
                    nmm = ceil_div(krange, 512)
                    w = [("prep", kb.v("prep")), ("ex", kb.v("ex")),
                         ("vte", kb.v("vte"))]
                    last_bank = (krange - 1) // 512
                    for kbm in range(nmm):
                        n0 = kbm * 512
                        n1 = min(krange, n0 + 512)
                        kb.op("tensor", "matmul", psA[:, n0:n1],
                              tile[:, gqt * 128:(gqt + 1) * 128], kg[:, n0:n1],
                              start=True, stop=(kbm != last_bank),
                              w=tuple(w) if kbm == 0 else ())
                    kb.op("tensor", "matmul", psA[:, gqt * 128:krange],
                          maskTb[:], identb[:], start=False, stop=True,
                          i=(("sct", 1),))

                def emit_exp(qt):
                    krange = (qt + 1) * 128
                    pp = qt % 2
                    kb.op("scalar", "activation", p_sb[pp][:, 0:krange],
                          psA[:, 0:krange], AF.Exp,
                          bias=negc[:, head:head + 1], scale=INV_SQRT_HD,
                          accum_out=lsum[pp][:],
                          w=(("sct", kb.v("sct")), ("dg", kb.v("dg") - 1),
                             ("ptt", kb.v("ptt"))),
                          i=(("ex", 1),))
                    kb.op("vector", "reciprocal", linv[pp][:], lsum[pp][:],
                          w=(("ex", kb.v("ex")),), i=(("dg", 1),))
                    kb.op("vector", "tensor_scalar", diagb[pp][:], identb[:],
                          linv[pp][:, 0:1], None, ALU.mult,
                          w=(("dg", kb.v("dg")),), i=(("dg", 1),))

                def emit_trans(qt):
                    gqt = qt
                    pp = qt % 2
                    dgt = kb.v("dg") if qt == NQT - 1 else kb.v("dg") - 2
                    done = 0
                    while done <= gqt:
                        cnt = min(4, gqt + 1 - done)
                        tb = (0, 1024, 1536)[kb.v("ptg") % 3]
                        kb.bump("ptg", 1)
                        w2 = [("dg", dgt), ("ptv", kb.v("ptv") - 2)]
                        for j in range(cnt):
                            kb2 = done + j
                            kb.op("tensor", "matmul",
                                  psB[:, tb + j * 128:tb + (j + 1) * 128],
                                  p_sb[pp][:, kb2 * 128:(kb2 + 1) * 128], diagb[pp][:],
                                  start=(j == 0), stop=(j == cnt - 1),
                                  w=tuple(w2) if j == 0 else (),
                                  i=(("ptt", 1),) if j == cnt - 1 else ())
                        dst_ap = ap3(stag, done * 512 + (qt % 4) * 128,
                                     [[512, cnt], [1, 128]])
                        src_ap = psB[:, tb:tb + cnt * 128].rearrange(
                            "p (a c) -> p a c", c=128)
                        kb.op("vector", "tensor_copy", dst_ap, src_ap,
                              w=(("ptt", kb.v("ptt")),), i=(("ptv", 1),))
                        done += cnt

                def emit_zero(B):
                    nkt = 4 * (B + 1)
                    w = [("av", kb.v("av"))]
                    for qt in range(4):
                        for kb2 in range(4 * B + qt + 1, nkt):
                            kb.op("gpsimd", "memset",
                                  stag[:, kb2 * 512 + qt * 128:kb2 * 512 + (qt + 1) * 128],
                                  0.0, w=tuple(w), i=(("pz", 1),))
                            w = []

                def emit_av(B):
                    nkt = 4 * (B + 1)
                    wv = [("ptv", kb.v("ptv")), ("pz", kb.v("pz")),
                          ("yv", kb.v("yv")), ("vte", kb.v("vte"))]
                    for kb2 in range(nkt):
                        kb.op("tensor", "matmul", psB[:, 512:1024],
                              v_tm[:, kb2 * 128:(kb2 + 1) * 128],
                              stag[:, kb2 * 512:(kb2 + 1) * 512],
                              start=(kb2 == 0), stop=(kb2 == nkt - 1),
                              w=tuple(wv) if kb2 == 0 else (),
                              i=(("av", 1),) if kb2 == nkt - 1 else ())
                    yslot, yfrees = r_ystg.next()
                    kb.op("vector", "tensor_copy", ystg[yslot][:], psB[:, 512:1024],
                          w=tuple([("av", kb.v("av"))] + list(yfrees)), i=(("yv", 1),))
                    kb.op("scalar", "dma_start",
                          out=yD[seq, head, :, B * 512:(B + 1) * 512],
                          in_=ystg[yslot][:], w=(("yv", kb.v("yv")),),
                          i=((f"sys{yslot}", 16),))
                    r_ystg.set_free(yslot, f"sys{yslot}", kb.v(f"sys{yslot}"))

                emit_zero(0)
                emit_scores(0)
                emit_exp(0)
                for qt in range(NQT):
                    if qt % 4 == 0 and qt > 0:
                        emit_zero(qt // 4)
                    if qt + 1 < NQT:
                        emit_scores(qt + 1)
                        emit_exp(qt + 1)
                    emit_trans(qt)
                    if qt % 4 == 3:
                        emit_av(qt // 4)
        ys_fins = [("sys0", kb.v("sys0")), ("sys1", kb.v("sys1"))]
        for n in range(4 if attn_oproj else 0):
            _, yfrees = r_yn.next()
            for kt in range(KT):
                w = list(ys_fins) + list(yfrees) if kt == 0 else []
                kb.op("sync", "dma_start", out=yn[kt][:],
                      in_=yD[seq, kt, :, n * 512:(n + 1) * 512],
                      w=tuple(w), i=(("ylo", 16),))
            ylo_t = kb.v("ylo")
            for ms in range(16):
                slot, wsem, wv = wload(weff[32 + ms], cast=False)
                pj = kb.v("ott")
                ps = psA[:, (pj % 2) * 512:(pj % 2) * 512 + 512]
                w = [(wsem, wv), ("ylo", ylo_t), ("ove", pj - 1),
                     ("ex", kb.v("ex")), ("pje", kb.v("pje"))]
                for kt in range(KT):
                    kb.op("tensor", "matmul", ps,
                          wst[slot][:, kt * 128:(kt + 1) * 128], yn[kt][:],
                          start=(kt == 0), stop=(kt == KT - 1),
                          w=tuple(w) if kt == 0 else (),
                          i=(("ott", 1),) if kt == KT - 1 else ())
                r_wst.set_free(slot, "ott", kb.v("ott"))
                rslot, rfrees = r_rfo.next()
                kb.op("sync", "dma_start", out=rfo[rslot][:],
                      in_=res[seq][ms * 128:(ms + 1) * 128, n * 512:(n + 1) * 512],
                      w=tuple(rfrees), i=((f"sro{rslot}", 16),))
                oslot, ofrees = r_ofo.next()
                kb.op("vector", "scalar_tensor_tensor", ofo[oslot][:], ps, 1.0,
                      rfo[rslot][:], ALU.bypass, ALU.add,
                      w=tuple([("ott", kb.v("ott")),
                               (f"sro{rslot}", kb.v(f"sro{rslot}"))] + list(ofrees)),
                      i=(("ove", 1),))
                r_rfo.set_free(rslot, "ove", kb.v("ove"))
                kb.op("scalar", "dma_start",
                      out=dst[seq][ms * 128:(ms + 1) * 128, n * 512:(n + 1) * 512],
                      in_=ofo[oslot][:], w=(("ove", kb.v("ove")),),
                      i=((f"sov{oslot}", 16),))
                r_ofo.set_free(oslot, f"sov{oslot}", kb.v(f"sov{oslot}"))
            r_yn.set_free(0, "ott", kb.v("ott"))
        kb.op("vector", "engine_nop", w=(("ove", kb.v("ove")),), i=(("vchunk", 1),))

    emit_setup()
    if "weff" in do:
        emit_weff()
    st["wl_sem"], st["wc_sem"] = "wl1", "wc1"
    if "ffn1" in do:
        for ci in range(ffn_chunks):
            emit_ffn_chunk(ci, [xT[s] for s in range(NSEQ)],
                           [uT[s] for s in range(NSEQ)], "sou")
    if "attn" in do:
        for seq in range(attn_seqs):
            emit_attn_seq(seq, [uT[s] for s in range(NSEQ)],
                          [vT[s] for s in range(NSEQ)],
                          [uT[s] for s in range(NSEQ)])
    st["wl_sem"], st["wc_sem"] = "wl2", "wc2"
    if "ffn2" in do:
        for ci in range(4):
            for eng in ("sync", "gpsimd", "vector", "scalar"):
                kb.wait(eng, "sov0", kb.v("sov0"))
                kb.wait(eng, "sov1", kb.v("sov1"))
            emit_ffn_chunk(ci, [vT[s] for s in range(NSEQ)],
                           [oT[s] for s in range(NSEQ)], "soo")
    if y_dbg is not None:
        kb.wait("sync", "ys", kb.v("ys"))
        kb.op("sync", "dma_start", out=y_dbg, in_=yD, i=(("tapst", 16),))
        kb.wait("sync", "tapst", kb.v("tapst"))
    for name, (dstap, srcap) in tap_ts.items():
        gates = {"uT": [("sou0", kb.v("sou0")), ("sou1", kb.v("sou1"))],
                 "vT": [("sov0", kb.v("sov0")), ("sov1", kb.v("sov1"))],
                 "weff": [(f"sws{j}", kb.v(f"sws{j}")) for j in range(4)]}[name]
        kb.op("sync", "dma_start", out=dstap, in_=srcap, w=tuple(gates),
              i=(("tapst", 16),))
    fin = [("soo0", kb.v("soo0")), ("soo1", kb.v("soo1"))]
    if tap_ts:
        fin.append(("tapst", kb.v("tapst")))
    for s, val in fin:
        kb.wait("sync", s, val)
        kb.wait("tensor", s, val)

    kb.drain_dma_sems()
    with nc.Block() as block:
        block.sync(lambda e: [f(e) for f in kb.q["sync"]])
        block.tensor(lambda e: [f(e) for f in kb.q["tensor"]])
        block.vector(lambda e: [f(e) for f in kb.q["vector"]])
        block.scalar(lambda e: [f(e) for f in kb.q["scalar"]])
        block.gpsimd(lambda e: [f(e) for f in kb.q["gpsimd"]])
    return nc


def _pretile(w, S, KTn):
    return np.ascontiguousarray(
        w.reshape(S, 128, KTn, 128).transpose(0, 3, 2, 1).reshape(S, 128, KTn * 128))


def host_inputs(inputs):
    f32 = np.float32
    gup = np.stack([_pretile(np.asarray(inputs["L0_ffn_gate_up_proj"][e], f32), 22, 16)
                    for e in range(E)])
    dnp = np.stack([_pretile(np.asarray(inputs["L0_ffn_down_proj"][e], f32), 16, 11)
                    for e in range(E)])
    srow = np.concatenate([np.asarray(inputs[f"L0_attn_s_{k}"], f32).reshape(1)
                           for k in "qkvo"] + [np.zeros(4, f32)]).reshape(1, 8)
    rsT = np.ascontiguousarray(np.concatenate(
        [np.asarray(inputs["L0_ffn_router_weight"], f32).T,
         np.asarray(inputs["L0_ffn_shared_gate_w"], f32).T], axis=1))
    identb = np.eye(128, dtype=np.float32).astype(ml_dtypes.bfloat16)
    inv_freq = 1.0 / (10000.0 ** (np.arange(0, 64, 2, dtype=np.float64) / 64.0))
    fr = np.outer(inv_freq, np.arange(SEQ, dtype=np.float64))
    cos = np.cos(fr).astype(f32)
    sin = np.sin(fr).astype(f32)
    T1 = np.concatenate([cos, sin], axis=0)
    T2 = np.concatenate([sin, cos], axis=0)
    trig = np.concatenate([T1, T2], axis=1).astype(ml_dtypes.bfloat16)
    sel = np.zeros((128, 64), np.float32)
    for j in range(32):
        sel[j, j] = 1.0
        sel[j + 32, j] = 1.0
        sel[64 + j, 32 + j] = -1.0
        sel[64 + 32 + j, 32 + j] = 1.0
    common = dict(
        selrope=sel.astype(ml_dtypes.bfloat16),
        gu_pre=gup, dn_pre=dnp,
        shg_pre=_pretile(np.asarray(inputs["L0_ffn_sh_gate_w"], f32), 44, 16),
        shu_pre=_pretile(np.asarray(inputs["L0_ffn_sh_up_w"], f32), 44, 16),
        shd_pre=_pretile(np.asarray(inputs["L0_ffn_sh_down_w"], f32), 16, 44),
        wq_pre=_pretile(np.asarray(inputs["L0_attn_W_q"], f32), 16, 16),
        wk_pre=_pretile(np.asarray(inputs["L0_attn_W_k"], f32), 8, 16),
        wv_pre=_pretile(np.asarray(inputs["L0_attn_W_v"], f32), 8, 16),
        wo_pre=_pretile(np.asarray(inputs["L0_attn_W_o"], f32), 16, 16),
        a_q=np.asarray(inputs["L0_attn_A_q"], f32),
        a_k=np.asarray(inputs["L0_attn_A_k"], f32),
        a_v=np.asarray(inputs["L0_attn_A_v"], f32),
        a_o=np.asarray(inputs["L0_attn_A_o"], f32),
        bT_q=np.ascontiguousarray(np.asarray(inputs["L0_attn_B_q"], f32).T),
        bT_k=np.ascontiguousarray(np.asarray(inputs["L0_attn_B_k"], f32).T),
        bT_v=np.ascontiguousarray(np.asarray(inputs["L0_attn_B_v"], f32).T),
        bT_o=np.ascontiguousarray(np.asarray(inputs["L0_attn_B_o"], f32).T),
        srow=srow, grow=np.asarray(inputs["L0_attn_q_gain"], f32).reshape(1, NH),
        rsT=rsT, identb=identb, ident9=np.eye(9, dtype=f32), trig=trig,
        maskadd=np.ascontiguousarray(np.triu(np.full((128, 128), -1e30, f32), k=1)),
        maskTb=np.ascontiguousarray(
            np.tril(np.full((128, 128), -1e30, f32), k=-1)).astype(ml_dtypes.bfloat16),
    )
    x = np.asarray(inputs["L0_attn_x"], f32)
    xT = np.ascontiguousarray(x.transpose(0, 2, 1))
    return common, xT


def kernel(**inputs):
    common, xT = host_inputs(inputs)
    nc = build_program()
    in_maps = []
    for c in range(NCORE):
        m = dict(common)
        m["xT"] = xT[2 * c:2 * c + 2]
        in_maps.append(m)
    res = run_bass_kernel_spmd(nc, in_maps, core_ids=list(range(NCORE)))
    out = np.concatenate([r["oT"] for r in res.results], axis=0)
    return np.ascontiguousarray(out.transpose(0, 2, 1))


# version 06
# speedup vs baseline: 1.1968x; 1.1948x over previous
import math
import numpy as np
import ml_dtypes
import concourse.bass as bass
import concourse.mybir as mybir
from concourse.bass_utils import run_bass_kernel_spmd

F32 = mybir.dt.float32
BF16 = mybir.dt.bfloat16
I8 = mybir.dt.int8
AF = mybir.ActivationFunctionType
ALU = mybir.AluOpType
AX = mybir.AxisListType

D = 2048
SEQ = 2048
NSEQ = 2
NCORE = 8
TC = 1024
KT = D // 128
E = 8
NH = 16
HD = 128
R = 32
INV_SQRT_HD = 1.0 / math.sqrt(HD)
EPS_LN = 1e-5
EPS_RMS = 1.1920929e-07
ENGINES = ("sync", "tensor", "vector", "scalar", "gpsimd")


class KB:
    def __init__(self, nc):
        self.nc = nc
        self.q = {e: [] for e in ENGINES}
        self.sems = {}
        self.cnt = {}
        self.max_wait = {}
        self.is_dma = {}

    def sem(self, name):
        if name not in self.sems:
            self.sems[name] = self.nc.alloc_semaphore(name)
        return self.sems[name]

    def v(self, name):
        return self.cnt.get(name, 0)

    def bump(self, name, n):
        self.cnt[name] = self.v(name) + n
        return self.cnt[name]

    serial = False
    last_inc = None

    def op(self, eng, meth, *a, w=(), i=(), **kw):
        w = list(w)
        i = list(i)
        if self.serial:
            if self.last_inc is not None:
                w.append(self.last_inc)
            if not i:
                i.append(("ser", 1))
            s0, n0 = i[0]
            self.last_inc = (s0, self.v(s0) + n0)
        for s, val in w:
            if s is not None and val > 0:
                self.max_wait[s] = max(self.max_wait.get(s, 0), val)
        if meth == "dma_start":
            for s, n in i:
                self.is_dma[s] = True
        tgts = [(self.sem(s), val) for s, val in w if s is not None and val > 0]
        incs = [(self.sem(s), n) for s, n in i]
        for s, n in i:
            self.bump(s, n)

        def f(e, meth=meth, a=a, kw=kw, tgts=tgts, incs=incs):
            for h, val in tgts:
                e.wait_ge(h, val)
            ins = getattr(e, meth)(*a, **kw)
            for h, n in incs:
                ins.then_inc(h, n)
        self.q[eng].append(f)

    def wait(self, eng, s, val):
        if val <= 0:
            return
        self.max_wait[s] = max(self.max_wait.get(s, 0), val)
        h = self.sem(s)
        self.q[eng].append(lambda e, h=h, v=val: e.wait_ge(h, v))

    def drain_dma_sems(self):
        for s in sorted(self.is_dma):
            if self.max_wait.get(s, 0) < self.v(s):
                self.wait("sync", s, self.v(s))


class Rot:

    def __init__(self, n):
        self.n = n
        self.free = [None] * n
        self.sticky = {}
        self.i = 0

    def next(self):
        slot = self.i % self.n
        self.i += 1
        rec = list(self.free[slot]) if self.free[slot] else [(None, 0)]
        if slot in self.sticky:
            rec.append(self.sticky.pop(slot))
        return slot, rec

    def set_free(self, slot, sem, val):
        self.free[slot] = [(sem, val)]

    def set_sticky(self, slot, sem, val):
        self.sticky[slot] = (sem, val)


def ceil_div(a, b):
    return (a + b - 1) // b


def ap3(base, elem_off, dims):
    part = [base.ap[0][0], base.ap[0][1]]
    return bass.AP(base.tensor, base.offset + elem_off, [part] + [list(d) for d in dims])


def build_program(taps=(), do=("weff", "ffn1", "attn", "ffn2"), attn_seqs=2, attn_groups=8, attn_blocks=4, attn_oproj=True, attn_stage=9, ffn_chunks=4, ffn_stage=9, ln_serial=0):
    nc = bass.Bass("TRN2", target_bir_lowering=False)
    ap = lambda t: t.ap()

    xT = ap(nc.dram_tensor("xT", [NSEQ, D, SEQ], F32, kind="ExternalInput"))
    gu_pre = ap(nc.dram_tensor("gu_pre", [E, 22, 128, KT * 128], F32, kind="ExternalInput"))
    dn_pre = ap(nc.dram_tensor("dn_pre", [E, 16, 128, 11 * 128], F32, kind="ExternalInput"))
    shg_pre = ap(nc.dram_tensor("shg_pre", [44, 128, KT * 128], F32, kind="ExternalInput"))
    shu_pre = ap(nc.dram_tensor("shu_pre", [44, 128, KT * 128], F32, kind="ExternalInput"))
    shd_pre = ap(nc.dram_tensor("shd_pre", [16, 128, 44 * 128], F32, kind="ExternalInput"))
    wq_pre = ap(nc.dram_tensor("wq_pre", [16, 128, KT * 128], F32, kind="ExternalInput"))
    wk_pre = ap(nc.dram_tensor("wk_pre", [8, 128, KT * 128], F32, kind="ExternalInput"))
    wv_pre = ap(nc.dram_tensor("wv_pre", [8, 128, KT * 128], F32, kind="ExternalInput"))
    wo_pre = ap(nc.dram_tensor("wo_pre", [16, 128, KT * 128], F32, kind="ExternalInput"))
    a_d = {k: ap(nc.dram_tensor(f"a_{k}", [R, D], F32, kind="ExternalInput")) for k in "qkvo"}
    bT_d = {k: ap(nc.dram_tensor(f"bT_{k}", [R, D if k in "qo" else D // 2], F32,
                                 kind="ExternalInput")) for k in "qkvo"}
    srow_d = ap(nc.dram_tensor("srow", [1, 8], F32, kind="ExternalInput"))
    grow_d = ap(nc.dram_tensor("grow", [1, NH], F32, kind="ExternalInput"))
    rsT_d = ap(nc.dram_tensor("rsT", [D, 9], F32, kind="ExternalInput"))
    identb_d = ap(nc.dram_tensor("identb", [128, 128], BF16, kind="ExternalInput"))
    ident9_d = ap(nc.dram_tensor("ident9", [9, 9], F32, kind="ExternalInput"))
    trig_d = ap(nc.dram_tensor("trig", [64, 2 * SEQ], BF16, kind="ExternalInput"))
    mask_d = ap(nc.dram_tensor("maskadd", [128, 128], F32, kind="ExternalInput"))
    selrope_d = ap(nc.dram_tensor("selrope", [128, 64], BF16, kind="ExternalInput"))
    maskTb_d = ap(nc.dram_tensor("maskTb", [128, 128], BF16, kind="ExternalInput"))

    uT = ap(nc.dram_tensor("uT", [NSEQ, D, SEQ], F32, kind="Internal"))
    vT = ap(nc.dram_tensor("vT", [NSEQ, D, SEQ], F32, kind="Internal"))
    weff = ap(nc.dram_tensor("weff", [48, 128, KT * 128], BF16, kind="Internal"))
    wbf_scr = ap(nc.dram_tensor("wbf_scr", [456, 128, KT * 128], BF16, kind="Internal"))
    yD = ap(nc.dram_tensor("yD", [NSEQ, 16, 128, SEQ], BF16, kind="Internal"))
    oT = ap(nc.dram_tensor("oT", [NSEQ, D, SEQ], F32, kind="ExternalOutput"))
    qk_dbg = None
    y_dbg = None
    h_dbg = None
    w_dbg = None
    f_dbg = None
    if "f0" in taps:
        f_dbg = ap(nc.dram_tensor("f_dbg", [16, 128, TC], F32, kind="ExternalOutput"))
    if "w0" in taps:
        w_dbg = ap(nc.dram_tensor("w_dbg", [128, 72], BF16, kind="ExternalOutput"))
    if "h0" in taps:
        h_dbg = ap(nc.dram_tensor("h_dbg", [KT, 128, TC], BF16, kind="ExternalOutput"))
    if "qk" in taps:
        qk_dbg = ap(nc.dram_tensor("qk_dbg", [4, 128, SEQ], BF16, kind="ExternalOutput"))
    if "yD" in taps:
        y_dbg = ap(nc.dram_tensor("y_dbg", [NSEQ, 16, 128, SEQ], BF16,
                                  kind="ExternalOutput"))
    tap_ts = {}
    for t in taps:
        if t in ("qk", "yD", "h0", "w0", "f0"):
            continue
        if t == "weff":
            tap_ts[t] = (ap(nc.dram_tensor("weff_dbg", [48, 128, KT * 128], BF16,
                                           kind="ExternalOutput")), weff)
        else:
            srct = {"uT": uT, "vT": vT}[t]
            tap_ts[t] = (ap(nc.dram_tensor(t + "_dbg", [NSEQ, D, SEQ], F32,
                                           kind="ExternalOutput")), srct)

    kb = KB(nc)

    def sb(name, shape, dtype):
        return nc.alloc_sbuf_tensor(name, shape, dtype)

    wst = [sb(f"wst{i}", [128, 2048], BF16) for i in range(4)]
    trig = sb("trig_s", [64, 2 * SEQ], BF16)
    skin = sb("skin", [1, 3072], F32)
    xf = [sb(f"xf{i}", [128, TC], F32) for i in range(2)]
    xb = [sb(f"xb{i}", [128, TC], BF16) for i in range(2)]
    sqb = [sb(f"sqb{i}", [128, TC], BF16) for i in range(2)]
    tn = sb("tn", [128, TC], F32)
    identb = sb("identb_s", [128, 128], BF16)
    ident9 = sb("ident9_s", [9, 9], F32)
    ones_rb = sb("ones_rb", [1, 128], BF16)
    ones_rf = sb("ones_rf", [1, 128], F32)
    ones_cb = sb("ones_cb", [128, 1], BF16)
    maskadd = sb("maskadd_s", [128, 128], F32)
    rs_bf = sb("rs_bf", [128, KT * 9], BF16)
    grow = sb("grow_s", [1, NH], F32)
    srow = sb("srow_s", [1, 8], F32)
    negc = sb("negc_s", [128, NH], F32)
    scol = sb("scol_s", [128, 8], F32)
    b30 = sb("b30", [128, 1], F32)
    beps_ln = sb("beps_ln", [128, 1], F32)
    beps_rms = sb("beps_rms", [128, 1], F32)
    wsig = sb("wsig", [128, 72], F32)
    wsig_bf = sb("wsig_bf", [128, 72], BF16)
    wrow_tmp = sb("wrow_tmp", [1, TC], BF16)
    lg9 = sb("lg9", [9, 128], F32)
    lgt = sb("lgt", [128, 9], F32)
    topt = sb("topt", [128, 9], F32)
    m1c = sb("m1c", [128, 1], F32)
    m2c = sb("m2c", [128, 1], F32)
    nm1 = sb("nm1", [128, 1], F32)
    dsum = sb("dsum", [128, 1], F32)
    dinv = sb("dinv", [128, 1], F32)
    lsum = [sb(f"lsum{i}", [128, 1], F32) for i in range(2)]
    linv = [sb(f"linv{i}", [128, 1], F32) for i in range(2)]
    diagb = [sb(f"diagb{i}", [128, 128], BF16) for i in range(2)]
    rsq = [sb(f"rsq{i}", [128, 512], BF16) for i in range(2)]
    ropA = sb("ropA", [128, SEQ], BF16)
    selrope = sb("selrope_s", [128, 64], BF16)
    maskTb = sb("maskTb_s", [128, 128], BF16)
    ystg = [sb(f"ystg{i}", [128, 512], BF16) for i in range(2)]

    BIGB = 132 * 1024
    big_h = nc.alloc_sbuf_tensor("big", [128, BIGB], I8)
    views = {F32: big_h.bitcast(F32), BF16: big_h.bitcast(BF16)}

    def at(byte_off, parts, cols, dtype):
        sz = 4 if dtype is F32 else 2
        assert byte_off % 64 == 0 and byte_off + cols * sz <= BIGB
        return views[dtype][0:parts, byte_off // sz: byte_off // sz + cols]

    KI = 1024
    f_sb = [at(ms * 4 * TC, 128, TC, F32) for ms in range(16)]
    hsb = [at(64 * KI + kt * 2 * TC, 128, TC, BF16) for kt in range(KT)]
    act = [at(96 * KI + s * 2 * TC, 128, TC, BF16) for s in range(11)]
    silu_b = [at(118 * KI + i * 2 * TC, 128, TC, BF16) for i in range(2)]
    wb_sb = at(122 * KI, 128, TC, BF16)
    off_b = [at(124 * KI + i * 2048, 128, 512, F32) for i in range(2)]
    rff = [at(128 * KI + i * 2048, 128, 512, F32) for i in range(2)]
    wtmp = [at(0 + i * 8192, 128, 2048, F32) for i in range(2)]
    _ab_off = 16 * 1024
    a_bf = {}
    bT_bf = {}
    for _i, _k in enumerate("qkvo"):
        a_bf[_k] = at(_ab_off, R, D, BF16)
        _ab_off += 2 * D
        _w = D if _k in "qo" else D // 2
        bT_bf[_k] = at(_ab_off, R, _w, BF16)
        _ab_off += 2 * _w
    h_full = [at(kt * 2 * SEQ, 128, SEQ, BF16) for kt in range(KT)]
    qg = [at(64 * KI + j * 2 * SEQ, 128, SEQ, BF16) for j in range(2)]
    kg = at(72 * KI, 128, SEQ, BF16)
    vg = at(76 * KI, 128, SEQ, BF16)
    v_tm = at(80 * KI, 128, KT * 128, BF16)
    p_sb = [at(84 * KI, 128, SEQ, BF16), at(104 * KI, 128, SEQ, BF16)]
    stag = at(88 * KI, 128, 16 * 512, BF16)
    yn = [at(64 * KI + kt * 1024, 128, 512, BF16) for kt in range(KT)]
    ofo = [at(80 * KI + i * 2048, 128, 512, F32) for i in range(2)]
    rfo = [at(84 * KI + i * 2048, 128, 512, F32) for i in range(2)]

    psA = nc.alloc_psum_tensor("psA", [128, 2048], F32).ap()
    psB = nc.alloc_psum_tensor("psB", [128, 2048], F32).ap()

    r_wst = Rot(4)
    r_xf = Rot(2)
    r_xb = Rot(2)
    r_off = Rot(2)
    r_rff = Rot(2)
    r_ystg = Rot(2)
    r_yn = Rot(1)
    r_ofo = Rot(2)
    r_rfo = Rot(2)
    r_silu = Rot(2)
    st = {"wl_sem": "wl1", "wc_sem": "wc1"}

    def wload(src_ap, cast=True):
        slot, frees = r_wst.next()
        sem = f"wl{'c' if cast else 'p'}{slot}"
        kb.op("gpsimd" if cast else "sync", "dma_start",
              out=wst[slot][:, 0:src_ap.shape[-1]], in_=src_ap,
              w=tuple(frees), i=((sem, 16),))
        return slot, sem, kb.v(sem)

    def wload_ffn(fp32_ap, sid, first):
        width = fp32_ap.shape[-1]
        if first:
            slot, sem, v = wload(fp32_ap, cast=True)
            kb.op("sync", "dma_start", out=wbf_scr[sid][:, 0:width],
                  in_=wst[slot][:, 0:width], w=((sem, v),),
                  i=((f"scb{slot}", 16),))
            r_wst.set_sticky(slot, f"scb{slot}", kb.v(f"scb{slot}"))
            st.setdefault("scb_rec", {})[sid] = (f"scb{slot}", kb.v(f"scb{slot}"))
            return slot, sem, v
        slot, frees = r_wst.next()
        sem = f"wlp{slot}"
        w = list(frees) + [st["scb_rec"][sid]]
        kb.op("sync", "dma_start", out=wst[slot][:, 0:width],
              in_=wbf_scr[sid][:, 0:width], w=tuple(w), i=((sem, 16),))
        return slot, sem, kb.v(sem)

    def emit_setup():
        for dst, src in [(identb, identb_d), (ident9, ident9_d), (trig, trig_d),
                         (maskadd, mask_d), (grow, grow_d), (srow, srow_d),
                         (selrope, selrope_d), (maskTb, maskTb_d)]:
            kb.op("sync", "dma_start", out=dst[:], in_=src, i=(("cl", 16),))
        rsr = rsT_d.rearrange("(kt p) e -> p kt e", p=128)
        kb.op("gpsimd", "dma_start",
              out=rs_bf[:].rearrange("p (kt e) -> p kt e", e=9), in_=rsr,
              i=(("clg", 16),))
        for kk in "qkvo":
            kb.op("gpsimd", "dma_start", out=a_bf[kk][:], in_=a_d[kk], i=(("clg", 16),))
            kb.op("gpsimd", "dma_start", out=bT_bf[kk][:], in_=bT_d[kk], i=(("clg", 16),))
        nl = kb.v("cl")
        kb.op("vector", "memset", ones_rb[:], 1.0)
        kb.op("vector", "memset", ones_rf[:], 1.0)
        kb.op("vector", "memset", ones_cb[:], 1.0)
        kb.op("vector", "memset", b30[:], 30.0)
        kb.op("vector", "memset", beps_ln[:], EPS_LN)
        kb.op("vector", "memset", beps_rms[:], EPS_RMS, i=(("cv", 1),))
        kb.op("scalar", "activation", skin[0:1, 0:NH], grow[0:1, :], AF.Abs,
              w=(("cl", nl),), i=(("cs", 1),))
        kb.op("vector", "tensor_scalar", skin[0:1, 64:64 + NH], skin[0:1, 0:NH],
              -math.sqrt(HD), 30.0, ALU.mult, ALU.add,
              w=(("cs", kb.v("cs")), ("cv", kb.v("cv"))), i=(("cv", 1),))
        kb.op("tensor", "matmul", psA[:, 0:NH], ones_rf[:], skin[0:1, 64:64 + NH],
              start=True, stop=True, w=(("cv", kb.v("cv")),), i=(("ct", 1),))
        kb.op("tensor", "matmul", psA[:, 64:72], ones_rf[:], srow[0:1, :],
              start=True, stop=True, i=(("ct", 1),))
        kb.op("vector", "tensor_copy", negc[:], psA[:, 0:NH], w=(("ct", 1),))
        kb.op("vector", "tensor_copy", scol[:], psA[:, 64:72], w=(("ct", 2),),
              i=(("vr", 1),))

    def emit_weff():
        specs = [("q", wq_pre, 16, 0, 0), ("k", wk_pre, 8, 16, 1),
                 ("v", wv_pre, 8, 24, 2), ("o", wo_pre, 16, 32, 3)]
        li = 0
        for kk, pre, ns, base, sidx in specs:
            for s in range(ns):
                tslot = li % 2
                w = [("wstt", li - 1)] if li - 1 > 0 else []
                kb.op("sync", "dma_start", out=wtmp[tslot][:], in_=pre[s],
                      w=tuple(w), i=((f"swt{tslot}", 16),))
                for kt in range(KT):
                    w2 = []
                    if kt == 0:
                        w2 = [("cl", kb.v("cl")), ("clg", kb.v("clg")), ("vr", kb.v("vr"))]
                        if li > 0:
                            w2.append(("wstt", li))
                    kb.op("tensor", "matmul", psA[:, kt * 128:(kt + 1) * 128],
                          a_bf[kk][:, kt * 128:(kt + 1) * 128],
                          bT_bf[kk][:, s * 128:(s + 1) * 128],
                          start=True, stop=True, w=tuple(w2),
                          i=(("bat", 1),) if kt == KT - 1 else ())
                wslot, frees = r_wst.next()
                kb.op("vector", "scalar_tensor_tensor", wst[wslot][:],
                      wtmp[tslot][:], scol[:, sidx:sidx + 1], psA[:, 0:2048],
                      ALU.mult, ALU.add,
                      w=tuple(list(frees) + [(f"swt{tslot}", kb.v(f"swt{tslot}")),
                              ("bat", kb.v("bat")), ("vr", kb.v("vr"))]),
                      i=(("wstt", 1),))
                kb.op("sync", "dma_start", out=weff[base + s], in_=wst[wslot][:],
                      w=(("wstt", kb.v("wstt")),), i=((f"sws{wslot}", 16),))
                r_wst.set_free(wslot, f"sws{wslot}", kb.v(f"sws{wslot}"))
                li += 1
        kb.op("vector", "engine_nop", w=(("wstt", kb.v("wstt")),), i=(("vchunk", 1),))

    def emit_ln(src_seq, t0, dst_fn):
        mask = getattr(kb, "ln_mask", 0)
        kb.serial = bool(mask & 1)
        for kt in range(KT):
            bslot, bfrees = r_xb.next()
            kb.op("gpsimd", "dma_start", out=xb[bslot][:],
                  in_=src_seq[kt * 128:(kt + 1) * 128, t0:t0 + TC],
                  w=tuple(bfrees), i=((f"sxb{bslot}", 16),))
            xblv = kb.v(f"sxb{bslot}")
            kb.op("vector", "tensor_mul", sqb[bslot][:], xb[bslot][:], xb[bslot][:],
                  w=tuple([(f"sxb{bslot}", xblv)] + list(bfrees)), i=(("sq", 1),))
            for n in range(2):
                kb.op("tensor", "matmul", psA[0:1, n * 512:(n + 1) * 512],
                      ones_cb[:], xb[bslot][:, n * 512:(n + 1) * 512],
                      start=(kt == 0), stop=(kt == KT - 1),
                      w=((f"sxb{bslot}", xblv),) if n == 0 else ())
            for n in range(2):
                kb.op("tensor", "matmul", psA[0:1, 1024 + n * 512:1024 + (n + 1) * 512],
                      ones_cb[:], sqb[bslot][:, n * 512:(n + 1) * 512],
                      start=(kt == 0), stop=(kt == KT - 1),
                      w=(("sq", kb.v("sq")),) if n == 0 else (),
                      i=(("stk", 1),) if n == 1 else ())
            r_xb.set_free(bslot, "stk", kb.v("stk"))
        kb.serial = bool(mask & 2)
        sk_m = skin[0:1, 0:TC]
        sk_b = skin[0:1, 1024:1024 + TC]
        sk_c = skin[0:1, 2048:2048 + TC]
        kb.op("scalar", "activation", sk_m, psA[0:1, 0:TC], AF.Copy, scale=1.0 / D,
              w=(("stk", kb.v("stk")), ("vln", kb.v("vln")), ("prt", kb.v("prt")),
                 ("prv", kb.v("prv"))),
              i=(("sk", 1),))
        kb.op("scalar", "activation", sk_b, psA[0:1, 1024:1024 + TC], AF.Copy,
              scale=1.0 / D, w=(("sk", kb.v("sk")),), i=(("sk", 1),))
        kb.op("vector", "tensor_mul", sk_c, sk_m, sk_m, w=(("sk", kb.v("sk")),),
              i=(("vln", 1),))
        kb.op("vector", "tensor_sub", sk_b, sk_b, sk_c, w=(("vln", kb.v("vln")),),
              i=(("vln", 1),))
        kb.op("scalar", "activation", sk_c, sk_b, AF.Ln, bias=beps_ln[0:1, 0:1],
              w=(("vln", kb.v("vln")),), i=(("sk", 1),))
        kb.op("scalar", "activation", sk_c, sk_c, AF.Exp, scale=-0.5,
              w=(("sk", kb.v("sk")),), i=(("sk", 1),))
        kb.op("vector", "scalar_tensor_tensor", sk_m, sk_m, -1.0, sk_c,
              ALU.mult, ALU.mult, w=(("sk", kb.v("sk")),), i=(("vln", 1),))
        kb.serial = bool(mask & 4)
        for n in range(2):
            kb.op("tensor", "matmul", psB[:, n * 512:(n + 1) * 512], ones_rf[:],
                  sk_c[0:1, n * 512:(n + 1) * 512], start=True, stop=True,
                  w=(("vln", kb.v("vln")),) if n == 0 else ())
        for n in range(2):
            kb.op("tensor", "matmul", psB[:, 1024 + n * 512:1024 + (n + 1) * 512],
                  ones_rf[:], sk_m[0:1, n * 512:(n + 1) * 512], start=True, stop=True,
                  i=(("bc", 1),) if n == 1 else ())
        for kt in range(KT):
            slot, frees = r_xf.next()
            kb.op("sync", "dma_start", out=xf[slot][:],
                  in_=src_seq[kt * 128:(kt + 1) * 128, t0:t0 + TC],
                  w=tuple(frees), i=((f"sxf{slot}", 16),))
            kb.op("vector", "tensor_mul", tn[:], xf[slot][:], psB[:, 0:TC],
                  w=((f"sxf{slot}", kb.v(f"sxf{slot}")), ("bc", kb.v("bc")),
                     ("h", kb.v("h"))),
                  i=(("t1", 1),))
            r_xf.set_free(slot, "t1", kb.v("t1"))
            kb.op("vector", "tensor_add", dst_fn(kt), tn[:], psB[:, 1024:1024 + TC],
                  w=(("t1", kb.v("t1")),), i=(("h", 1),))

    def emit_router(h_base):
        for kt in range(KT):
            for n in range(2):
                w = []
                if kt == 0 and n == 0:
                    w = [("h", h_base + KT), ("sk", kb.v("sk"))]
                kb.op("tensor", "matmul", psA[0:9, n * 512:(n + 1) * 512],
                      rs_bf[:, kt * 9:(kt + 1) * 9],
                      hsb[kt][:, n * 512:(n + 1) * 512],
                      start=(kt == 0), stop=(kt == KT - 1), w=tuple(w),
                      i=(("rt", 1),) if kt == KT - 1 and n == 1 else ())
        for stt in range(8):
            wt = psB[:, (stt % 2) * 512:(stt % 2) * 512 + 9]
            w = [("rt", kb.v("rt")), ("wtp", kb.v("wtp"))]
            kb.op("vector", "tensor_copy", lg9[:],
                  psA[0:9, stt * 128:(stt + 1) * 128], w=tuple(w), i=(("rte", 1),))
            kb.op("tensor", "matmul", wt, lg9[:], ident9[:], start=True, stop=True,
                  w=(("rte", kb.v("rte")), ("wv", kb.v("wv") - 11),
                     ("rs2", kb.v("rs2") - 2)), i=(("wtp", 1),))
            b = kb.v("wv")
            kb.op("vector", "tensor_copy", lgt[:], wt, w=(("wtp", kb.v("wtp")),),
                  i=(("wv", 1),))
            kb.op("vector", "reduce_max", m1c[:], lgt[:, 0:8], AX.X,
                  w=(("wv", b + 1),), i=(("wv", 1),))
            kb.op("vector", "tensor_scalar", topt[:, 0:8], lgt[:, 0:8], m1c[:, 0:1],
                  -1e30, ALU.is_equal, ALU.mult, w=(("wv", b + 2),), i=(("wv", 1),))
            kb.op("vector", "tensor_add", topt[:, 0:8], topt[:, 0:8], lgt[:, 0:8],
                  w=(("wv", b + 3),), i=(("wv", 1),))
            kb.op("vector", "reduce_max", m2c[:], topt[:, 0:8], AX.X,
                  w=(("wv", b + 4),), i=(("wv", 1),))
            kb.op("vector", "tensor_scalar", nm1[:], m1c[:], -1.0, None, ALU.mult,
                  w=(("wv", b + 5),), i=(("wv", 1),))
            kb.op("scalar", "activation", topt[:, 0:8], lgt[:, 0:8], AF.Exp,
                  bias=nm1[:, 0:1], w=(("wv", b + 6),), i=(("rs2", 1),))
            kb.op("vector", "tensor_scalar", lgt[:, 0:8], lgt[:, 0:8], m2c[:, 0:1],
                  None, ALU.is_ge, w=(("rs2", kb.v("rs2")),), i=(("wv", 1),))
            kb.op("vector", "tensor_mul", topt[:, 0:8], topt[:, 0:8], lgt[:, 0:8],
                  w=(("wv", b + 7),), i=(("wv", 1),))
            kb.op("vector", "reduce_sum", dsum[:], topt[:, 0:8], AX.X,
                  w=(("wv", b + 8),), i=(("wv", 1),))
            kb.op("vector", "reciprocal", dinv[:], dsum[:], w=(("wv", b + 9),),
                  i=(("wv", 1),))
            kb.op("vector", "tensor_scalar", wsig[:, stt * 9:stt * 9 + 8], topt[:, 0:8],
                  dinv[:, 0:1], None, ALU.mult, w=(("wv", b + 10),), i=(("wv", 1),))
            kb.op("scalar", "activation", wsig[:, stt * 9 + 8:stt * 9 + 9], wt[:, 8:9],
                  AF.Sigmoid, w=(("wtp", kb.v("wtp")),), i=(("rs2", 1),))
        kb.op("gpsimd", "tensor_copy", wsig_bf[:], wsig[:],
              w=(("wv", kb.v("wv")), ("rs2", kb.v("rs2"))), i=(("wvb", 1),))

    DGRP = None

    def emit_unit(u, gate_aps, up_aps, down_ap_fn, wcol, fa_base, u0_extra, first_chunk):
        DGRP = [(psB, 0), (psB, 1024), (psA, 0), (psA, 1024)]
        w0 = [("wvb", kb.v("wvb"))]
        if u >= 1:
            w0.append(("fa", fa_base + (u - 1) * 16 + 14))
        else:
            w0 += u0_extra
        for stt in range(8):
            kb.op("tensor", "matmul", psB[0:1, stt * 128:(stt + 1) * 128],
                  wsig_bf[:, stt * 9 + wcol:stt * 9 + wcol + 1], identb[:],
                  start=(stt in (0, 4)), stop=(stt in (3, 7)),
                  w=tuple(w0) if stt == 0 else (), i=(("wrt", 1),) if stt == 7 else ())
        kb.op("vector", "tensor_copy", wrow_tmp[0:1, :], psB[0:1, 0:TC],
              w=(("wrt", kb.v("wrt")),), i=(("wre", 1),))
        for n in range(2):
            kb.op("tensor", "matmul", psB[:, n * 512:(n + 1) * 512], ones_rb[:],
                  wrow_tmp[0:1, n * 512:(n + 1) * 512], start=True, stop=True,
                  w=(("wre", kb.v("wre")),) if n == 0 else (),
                  i=(("wbp", 1),) if n == 1 else ())
        kb.op("vector", "tensor_copy", wb_sb[:], psB[:, 0:TC],
              w=(("wbp", kb.v("wbp")),), i=(("wbv", 1),))
        si_base = kb.v("si")
        mu_base = kb.v("mu")
        am_base = kb.v("am")
        for s in range(11):
            gslot, gsem_g, gv = wload_ffn(gate_aps[s], u * 38 + 2 * s, first_chunk)
            uslot, gsem_u, uv = wload_ffn(up_aps[s], u * 38 + 2 * s + 1, first_chunk)
            ps = psA if s % 2 == 0 else psB
            wv = [(gsem_g, gv), (gsem_u, uv)]
            if s >= 2:
                wv += [("si", si_base + s - 1), ("mu", mu_base + s - 1)]
            if s == 1:
                wv += [("wbv", kb.v("wbv")),
                       ("fa", fa_base + (u - 1) * 16 + 14) if u >= 1 else (None, 0)]
            if s == 0 and u >= 1:
                wv += [("fa", fa_base + (u - 1) * 16 + 16)]
            first = True
            for kt in range(KT):
                for half, slot in ((0, gslot), (1, uslot)):
                    for n in range(2):
                        kb.op("tensor", "matmul",
                              ps[:, half * TC + n * 512: half * TC + (n + 1) * 512],
                              wst[slot][:, kt * 128:(kt + 1) * 128],
                              hsb[kt][:, n * 512:(n + 1) * 512],
                              start=(kt == 0), stop=(kt == KT - 1),
                              w=tuple(wv) if first else (),
                              i=(("gu", 1),)
                              if kt == KT - 1 and half == 1 and n == 1 else ())
                        first = False
            r_wst.set_free(gslot, "gu", kb.v("gu"))
            r_wst.set_free(uslot, "gu", kb.v("gu"))
            sslot, sfrees = r_silu.next()
            kb.op("scalar", "activation", silu_b[sslot][:], ps[:, 0:TC], AF.Silu,
                  w=tuple([("gu", kb.v("gu"))] + list(sfrees)), i=(("si", 1),))
            kb.op("vector", "tensor_mul", act[s][:], silu_b[sslot][:], ps[:, TC:2 * TC],
                  w=(("si", kb.v("si")),), i=(("mu", 1),))
            r_silu.set_free(sslot, "mu", kb.v("mu"))
            kb.op("vector", "tensor_mul", act[s][:], act[s][:], wb_sb[:],
                  w=(("mu", kb.v("mu")),), i=(("am", 1),))
        for ms in range(16):
            dslot, dsem, dv = wload_ffn(down_ap_fn(ms), u * 38 + 22 + ms, first_chunk)
            ps, c0 = DGRP[ms % 4]
            wv = [(dsem, dv)]
            if ms == 0:
                wv += [("am", am_base + 9), ("si", si_base + 10)]
            elif ms == 1:
                wv.append(("mu", mu_base + 10))
            elif ms == 2:
                wv.append(("si", si_base + 11))
            elif ms == 3:
                wv.append(("mu", mu_base + 11))
            else:
                wv.append(("fa", kb.v("fa") - 3))
            first = True
            for kt in range(11):
                if first:
                    kw = list(wv)
                elif ms == 0 and kt in (9, 10):
                    kw = [("am", am_base + kt + 1)]
                elif ms == 1 and kt == 10:
                    kw = [("am", am_base + 11)]
                else:
                    kw = []
                for n in range(2):
                    kb.op("tensor", "matmul", ps[:, c0 + n * 512:c0 + (n + 1) * 512],
                          wst[dslot][:, kt * 128:(kt + 1) * 128],
                          act[kt][:, n * 512:(n + 1) * 512],
                          start=(kt == 0), stop=(kt == 10),
                          w=tuple(kw) if n == 0 else (),
                          i=(("dn", 1),) if kt == 10 and n == 1 else ())
                    first = False
            r_wst.set_free(dslot, "dn", kb.v("dn"))
            if u == 0:
                kb.op("vector", "tensor_scalar", f_sb[ms][:], ps[:, c0:c0 + TC],
                      0.5, None, ALU.mult, w=(("dn", kb.v("dn")),), i=(("fa", 1),))
            else:
                kb.op("vector", "scalar_tensor_tensor", f_sb[ms][:],
                      ps[:, c0:c0 + TC], 0.5, f_sb[ms][:], ALU.mult, ALU.add,
                      w=(("dn", kb.v("dn")),), i=(("fa", 1),))

    def emit_ffn_chunk(ci, src, dst, sosem):
        _ffn_dst_marker = object()
        seq, half = divmod(ci, 2)
        t0 = half * TC
        h_base = kb.v("h")
        kb.wait("tensor", "vchunk", kb.v("vchunk"))
        kb.ln_mask = ln_serial
        emit_ln(src[seq], t0, lambda kt: hsb[kt][:])
        kb.ln_mask = 0
        kb.serial = False
        if h_dbg is not None and ci == 0 and dst is not _ffn_dst_marker:
            pass
        if h_dbg is not None and kb.v("hdumped") == 0:
            kb.wait("sync", "h", kb.v("h"))
            for kt in range(KT):
                kb.op("sync", "dma_start", out=h_dbg[kt], in_=hsb[kt][:],
                      i=(("tapst", 16),))
            kb.wait("sync", "tapst", kb.v("tapst"))
            kb.bump("hdumped", 1)
        if ffn_stage < 1:
            kb.op("vector", "engine_nop", w=(("h", kb.v("h")),), i=(("vchunk", 1),))
            return
        emit_router(h_base)
        fa_base = kb.v("fa")
        if w_dbg is not None and kb.v("wdumped") == 0:
            kb.wait("sync", "wvb", kb.v("wvb"))
            kb.op("sync", "dma_start", out=w_dbg, in_=wsig_bf[:], i=(("tapst", 16),))
            kb.wait("sync", "tapst", kb.v("tapst"))
            kb.bump("wdumped", 1)
        if ffn_stage < 2:
            kb.op("vector", "engine_nop", w=(("wv", kb.v("wv")),), i=(("vchunk", 1),))
            return
        u0_extra = [("rte", kb.v("rte")), ("t1", kb.v("t1"))]
        first = (kb.v("ffnchunks") == 0)
        kb.bump("ffnchunks", 1)
        for u in range(12):
            if u < 8:
                emit_unit(u, [gu_pre[u, s] for s in range(11)],
                          [gu_pre[u, 11 + s] for s in range(11)],
                          lambda ms, u=u: dn_pre[u, ms], u, fa_base, u0_extra, first)
            else:
                g = u - 8
                emit_unit(u, [shg_pre[g * 11 + s] for s in range(11)],
                          [shu_pre[g * 11 + s] for s in range(11)],
                          lambda ms, g=g: shd_pre[ms][:, g * 1408:(g + 1) * 1408],
                          8, fa_base, u0_extra, first)
        if f_dbg is not None and kb.v("fdumped") == 0:
            kb.wait("sync", "fa", kb.v("fa"))
            for ms in range(16):
                kb.op("sync", "dma_start", out=f_dbg[ms], in_=f_sb[ms][:],
                      i=(("tapst", 16),))
            kb.wait("sync", "tapst", kb.v("tapst"))
            kb.bump("fdumped", 1)
        for j in range(32):
            ms, hf = divmod(j, 2)
            rslot, rfrees = r_rff.next()
            kb.op("sync", "dma_start", out=rff[rslot][:],
                  in_=src[seq][ms * 128:(ms + 1) * 128, t0 + hf * 512:t0 + (hf + 1) * 512],
                  w=tuple(rfrees), i=((f"srf{rslot}", 16),))
            oslot, ofrees = r_off.next()
            kb.op("vector", "tensor_add", off_b[oslot][:],
                  f_sb[ms][:, hf * 512:(hf + 1) * 512], rff[rslot][:],
                  w=tuple([(f"srf{rslot}", kb.v(f"srf{rslot}")),
                           ("fa", fa_base + 11 * 16 + ms + 1)] + list(ofrees)),
                  i=(("f3", 1),))
            r_rff.set_free(rslot, "f3", kb.v("f3"))
            ssem = f"{sosem}{oslot}"
            kb.op("scalar", "dma_start",
                  out=dst[seq][ms * 128:(ms + 1) * 128, t0 + hf * 512:t0 + (hf + 1) * 512],
                  in_=off_b[oslot][:], w=(("f3", kb.v("f3")),), i=((ssem, 16),))
            r_off.set_free(oslot, ssem, kb.v(ssem))
        kb.op("vector", "engine_nop", w=(("f3", kb.v("f3")),), i=(("vchunk", 1),))

    def emit_prep(tile, head, is_q):
        rpe_pb = kb.v("rpe")
        for b in range(4):
            slot = kb.v("rsqs") % 2
            w = [("rst", kb.v("rst") - 1), ("prv", kb.v("prv")),
                 ("pje", kb.v("pje"))]
            kb.op("scalar", "activation", rsq[slot][:], tile[:, b * 512:(b + 1) * 512],
                  AF.Square, w=tuple(w), i=(("rsqs", 1),))
            w2 = [("rsqs", kb.v("rsqs"))]
            if b == 0:
                w2 += [("yv", kb.v("yv"))]
            if rpe_pb:
                w2 += [("rpe", rpe_pb - 3 + b)]
            kb.op("tensor", "matmul", psB[0:1, b * 512:(b + 1) * 512], ones_cb[:],
                  rsq[slot][:], start=True, stop=True, w=tuple(w2), i=(("rst", 1),))
        rst_f = kb.v("rst")
        sk_a = skin[0:1, 0:SEQ]
        for n in range(4):
            blk = sk_a[0:1, n * 512:(n + 1) * 512]
            w = [("rst", rst_f - 3 + n)]
            if n == 0:
                w += [("prt", kb.v("prt")), ("bc", kb.v("bc")),
                      ("vln", kb.v("vln"))]
            kb.op("scalar", "activation", blk, psB[0:1, n * 512:(n + 1) * 512],
                  AF.Ln, bias=beps_rms[0:1, 0:1], scale=1.0 / HD,
                  w=tuple(w), i=(("prs", 1),))
            kb.op("scalar", "activation", blk, blk, AF.Exp, scale=-0.5,
                  w=(("prs", kb.v("prs")),), i=(("prs", 1),))
            if is_q:
                kb.op("vector", "tensor_scalar", blk, blk,
                      grow[0:1, head:head + 1], None, ALU.mult,
                      w=(("prs", kb.v("prs")),), i=(("prv", 1),))
                wsc = ("prv", kb.v("prv"))
            else:
                wsc = ("prs", kb.v("prs"))
            kb.op("tensor", "matmul", psB[:, n * 512:(n + 1) * 512], ones_rf[:],
                  blk, start=True, stop=True, w=(wsc,), i=(("prt", 1),))
            kb.op("vector", "tensor_mul", tile[:, n * 512:(n + 1) * 512],
                  tile[:, n * 512:(n + 1) * 512], psB[:, n * 512:(n + 1) * 512],
                  w=(("prt", kb.v("prt")),), i=(("prv", 1),))
            kb.op("vector", "tensor_mul", ropA[0:64, n * 512:(n + 1) * 512],
                  tile[0:64, n * 512:(n + 1) * 512], trig[:, n * 512:(n + 1) * 512],
                  w=(("rpe", kb.v("rpe")),), i=(("prq", 1),))
            kb.op("gpsimd", "tensor_mul", ropA[64:128, n * 512:(n + 1) * 512],
                  tile[0:64, n * 512:(n + 1) * 512],
                  trig[:, SEQ + n * 512:SEQ + (n + 1) * 512],
                  w=(("prv", kb.v("prv")), ("rpe", kb.v("rpe"))), i=(("prp", 1),))
            kb.op("tensor", "matmul", psB[0:64, n * 512:(n + 1) * 512],
                  selrope[:], ropA[:, n * 512:(n + 1) * 512],
                  start=True, stop=True,
                  w=(("prp", kb.v("prp")), ("prq", kb.v("prq"))),
                  i=(("rpt", 1),))
            kb.op("vector", "tensor_copy", tile[0:64, n * 512:(n + 1) * 512],
                  psB[0:64, n * 512:(n + 1) * 512],
                  w=(("rpt", kb.v("rpt")),), i=(("rpe", 1),))
        kb.op("gpsimd", "engine_nop", w=(("rpe", kb.v("rpe")),), i=(("prep", 1),))

    def emit_attn_seq(seq, src, dst, res):
        kb.wait("tensor", "vchunk", kb.v("vchunk"))
        for eng in ("vector", "gpsimd", "scalar"):
            kb.wait(eng, "sov0", kb.v("sov0"))
            kb.wait(eng, "sov1", kb.v("sov1"))
        for half in range(2):
            for eng in ("sync", "gpsimd"):
                kb.wait(eng, "sou0", kb.v("sou0"))
                kb.wait(eng, "sou1", kb.v("sou1"))
            emit_ln(src[seq], half * TC,
                    lambda kt, half=half: h_full[kt][:, half * TC:(half + 1) * TC])
        h_fin = kb.v("h")
        for g in range(attn_groups):
            if attn_stage < 1:
                continue
            for widx, dtile in [(2 * g, qg[0]), (2 * g + 1, qg[1]), (16 + g, kg),
                                (24 + g, vg)]:
                slot, wsem, wv = wload(weff[widx], cast=False)
                for n in range(4):
                    pj = kb.v("pjt")
                    ps = psA[:, (pj % 2) * 512:(pj % 2) * 512 + 512]
                    w = [(wsem, wv), ("h", h_fin), ("pje", pj - 1),
                         ("ex", kb.v("ex")), ("sk", kb.v("sk"))]
                    for kt in range(KT):
                        kb.op("tensor", "matmul", ps,
                              wst[slot][:, kt * 128:(kt + 1) * 128],
                              h_full[kt][:, n * 512:(n + 1) * 512],
                              start=(kt == 0), stop=(kt == KT - 1),
                              w=tuple(w) if kt == 0 else (),
                              i=(("pjt", 1),) if kt == KT - 1 else ())
                    kb.op("vector", "tensor_copy", dtile[:, n * 512:(n + 1) * 512],
                          ps, w=(("pjt", kb.v("pjt")),), i=(("pje", 1),))
                r_wst.set_free(slot, "pjt", kb.v("pjt"))
            if attn_stage < 2:
                continue
            emit_prep(qg[0], 2 * g, True)
            emit_prep(qg[1], 2 * g + 1, True)
            emit_prep(kg, 0, False)
            if qk_dbg is not None and seq == 0 and g == 0:
                kb.wait("sync", "prep", kb.v("prep"))
                for _i, _t in enumerate([qg[0], qg[1], kg, vg]):
                    kb.op("sync", "dma_start", out=qk_dbg[_i], in_=_t[:],
                          i=(("tapst", 16),))
                kb.wait("sync", "tapst", kb.v("tapst"))
            if attn_stage < 3:
                continue
            for grp in range(4):
                w = [("prep", kb.v("prep")), ("vte", kb.v("vte")),
                     ("prv", kb.v("prv")), ("pje", kb.v("pje"))]
                for j in range(4):
                    kt = grp * 4 + j
                    kb.op("tensor", "matmul", psB[:, 1024 + j * 128:1024 + (j + 1) * 128],
                          vg[:, kt * 128:(kt + 1) * 128], identb[:],
                          start=(j == 0), stop=(j == 3),
                          w=tuple(w) if j == 0 else (),
                          i=(("vtt", 1),) if j == 3 else ())
                kb.op("vector", "tensor_copy", v_tm[:, grp * 512:(grp + 1) * 512],
                      psB[:, 1024:1536], w=(("vtt", kb.v("vtt")),), i=(("vte", 1),))
            if attn_stage < 4:
                continue
            for hh in range(2):
                head = 2 * g + hh
                tile = qg[hh]
                NQT = 4 * attn_blocks

                def emit_scores(qt):
                    gqt = qt
                    krange = (gqt + 1) * 128
                    nmm = ceil_div(krange, 512)
                    w = [("prep", kb.v("prep")), ("ex", kb.v("ex")),
                         ("vte", kb.v("vte"))]
                    last_bank = (krange - 1) // 512
                    for kbm in range(nmm):
                        n0 = kbm * 512
                        n1 = min(krange, n0 + 512)
                        kb.op("tensor", "matmul", psA[:, n0:n1],
                              tile[:, gqt * 128:(gqt + 1) * 128], kg[:, n0:n1],
                              start=True, stop=(kbm != last_bank),
                              w=tuple(w) if kbm == 0 else ())
                    kb.op("tensor", "matmul", psA[:, gqt * 128:krange],
                          maskTb[:], identb[:], start=False, stop=True,
                          i=(("sct", 1),))

                def emit_exp(qt):
                    krange = (qt + 1) * 128
                    pp = qt % 2
                    kb.op("scalar", "activation", p_sb[pp][:, 0:krange],
                          psA[:, 0:krange], AF.Exp,
                          bias=negc[:, head:head + 1], scale=INV_SQRT_HD,
                          accum_out=lsum[pp][:],
                          w=(("sct", kb.v("sct")), ("dg", kb.v("dg") - 1),
                             ("ptt", kb.v("ptt"))),
                          i=(("ex", 1),))
                    kb.op("vector", "reciprocal", linv[pp][:], lsum[pp][:],
                          w=(("ex", kb.v("ex")),), i=(("dg", 1),))
                    kb.op("vector", "tensor_scalar", diagb[pp][:], identb[:],
                          linv[pp][:, 0:1], None, ALU.mult,
                          w=(("dg", kb.v("dg")),), i=(("dg", 1),))

                def emit_trans(qt):
                    gqt = qt
                    pp = qt % 2
                    dgt = kb.v("dg") if qt == NQT - 1 else kb.v("dg") - 2
                    done = 0
                    while done <= gqt:
                        cnt = min(4, gqt + 1 - done)
                        tb = (0, 1024, 1536)[kb.v("ptg") % 3]
                        kb.bump("ptg", 1)
                        w2 = [("dg", dgt), ("ptv", kb.v("ptv") - 2)]
                        for j in range(cnt):
                            kb2 = done + j
                            kb.op("tensor", "matmul",
                                  psB[:, tb + j * 128:tb + (j + 1) * 128],
                                  p_sb[pp][:, kb2 * 128:(kb2 + 1) * 128], diagb[pp][:],
                                  start=(j == 0), stop=(j == cnt - 1),
                                  w=tuple(w2) if j == 0 else (),
                                  i=(("ptt", 1),) if j == cnt - 1 else ())
                        dst_ap = ap3(stag, done * 512 + (qt % 4) * 128,
                                     [[512, cnt], [1, 128]])
                        src_ap = psB[:, tb:tb + cnt * 128].rearrange(
                            "p (a c) -> p a c", c=128)
                        kb.op("vector", "tensor_copy", dst_ap, src_ap,
                              w=(("ptt", kb.v("ptt")),), i=(("ptv", 1),))
                        done += cnt

                def emit_zero(B):
                    nkt = 4 * (B + 1)
                    w = [("av", kb.v("av"))]
                    for qt in range(4):
                        for kb2 in range(4 * B + qt + 1, nkt):
                            kb.op("gpsimd", "memset",
                                  stag[:, kb2 * 512 + qt * 128:kb2 * 512 + (qt + 1) * 128],
                                  0.0, w=tuple(w), i=(("pz", 1),))
                            w = []

                def emit_av(B):
                    nkt = 4 * (B + 1)
                    wv = [("ptv", kb.v("ptv")), ("pz", kb.v("pz")),
                          ("yv", kb.v("yv")), ("vte", kb.v("vte"))]
                    for kb2 in range(nkt):
                        kb.op("tensor", "matmul", psB[:, 512:1024],
                              v_tm[:, kb2 * 128:(kb2 + 1) * 128],
                              stag[:, kb2 * 512:(kb2 + 1) * 512],
                              start=(kb2 == 0), stop=(kb2 == nkt - 1),
                              w=tuple(wv) if kb2 == 0 else (),
                              i=(("av", 1),) if kb2 == nkt - 1 else ())
                    yslot, yfrees = r_ystg.next()
                    kb.op("vector", "tensor_copy", ystg[yslot][:], psB[:, 512:1024],
                          w=tuple([("av", kb.v("av"))] + list(yfrees)), i=(("yv", 1),))
                    kb.op("scalar", "dma_start",
                          out=yD[seq, head, :, B * 512:(B + 1) * 512],
                          in_=ystg[yslot][:], w=(("yv", kb.v("yv")),),
                          i=((f"sys{yslot}", 16),))
                    r_ystg.set_free(yslot, f"sys{yslot}", kb.v(f"sys{yslot}"))

                emit_zero(0)
                emit_scores(0)
                emit_exp(0)
                for qt in range(NQT):
                    if qt % 4 == 0 and qt > 0:
                        emit_zero(qt // 4)
                    if qt + 1 < NQT:
                        emit_scores(qt + 1)
                        emit_exp(qt + 1)
                    emit_trans(qt)
                    if qt % 4 == 3:
                        emit_av(qt // 4)
        ys_fins = [("sys0", kb.v("sys0")), ("sys1", kb.v("sys1"))]
        for n in range(4 if attn_oproj else 0):
            _, yfrees = r_yn.next()
            for kt in range(KT):
                w = list(ys_fins) + list(yfrees) if kt == 0 else []
                kb.op("sync", "dma_start", out=yn[kt][:],
                      in_=yD[seq, kt, :, n * 512:(n + 1) * 512],
                      w=tuple(w), i=(("ylo", 16),))
            ylo_t = kb.v("ylo")
            for ms in range(16):
                slot, wsem, wv = wload(weff[32 + ms], cast=False)
                pj = kb.v("ott")
                ps = psA[:, (pj % 2) * 512:(pj % 2) * 512 + 512]
                w = [(wsem, wv), ("ylo", ylo_t), ("ove", pj - 1),
                     ("ex", kb.v("ex")), ("pje", kb.v("pje"))]
                for kt in range(KT):
                    kb.op("tensor", "matmul", ps,
                          wst[slot][:, kt * 128:(kt + 1) * 128], yn[kt][:],
                          start=(kt == 0), stop=(kt == KT - 1),
                          w=tuple(w) if kt == 0 else (),
                          i=(("ott", 1),) if kt == KT - 1 else ())
                r_wst.set_free(slot, "ott", kb.v("ott"))
                rslot, rfrees = r_rfo.next()
                kb.op("sync", "dma_start", out=rfo[rslot][:],
                      in_=res[seq][ms * 128:(ms + 1) * 128, n * 512:(n + 1) * 512],
                      w=tuple(rfrees), i=((f"sro{rslot}", 16),))
                oslot, ofrees = r_ofo.next()
                kb.op("vector", "scalar_tensor_tensor", ofo[oslot][:], ps, 1.0,
                      rfo[rslot][:], ALU.bypass, ALU.add,
                      w=tuple([("ott", kb.v("ott")),
                               (f"sro{rslot}", kb.v(f"sro{rslot}"))] + list(ofrees)),
                      i=(("ove", 1),))
                r_rfo.set_free(rslot, "ove", kb.v("ove"))
                kb.op("scalar", "dma_start",
                      out=dst[seq][ms * 128:(ms + 1) * 128, n * 512:(n + 1) * 512],
                      in_=ofo[oslot][:], w=(("ove", kb.v("ove")),),
                      i=((f"sov{oslot}", 16),))
                r_ofo.set_free(oslot, f"sov{oslot}", kb.v(f"sov{oslot}"))
            r_yn.set_free(0, "ott", kb.v("ott"))
        kb.op("vector", "engine_nop", w=(("ove", kb.v("ove")),), i=(("vchunk", 1),))

    emit_setup()
    if "weff" in do:
        emit_weff()
    st["wl_sem"], st["wc_sem"] = "wl1", "wc1"
    if "ffn1" in do:
        for ci in range(ffn_chunks):
            emit_ffn_chunk(ci, [xT[s] for s in range(NSEQ)],
                           [uT[s] for s in range(NSEQ)], "sou")
    if "attn" in do:
        for seq in range(attn_seqs):
            emit_attn_seq(seq, [uT[s] for s in range(NSEQ)],
                          [vT[s] for s in range(NSEQ)],
                          [uT[s] for s in range(NSEQ)])
    st["wl_sem"], st["wc_sem"] = "wl2", "wc2"
    if "ffn2" in do:
        for ci in range(4):
            for eng in ("sync", "gpsimd", "vector", "scalar"):
                kb.wait(eng, "sov0", kb.v("sov0"))
                kb.wait(eng, "sov1", kb.v("sov1"))
            emit_ffn_chunk(ci, [vT[s] for s in range(NSEQ)],
                           [oT[s] for s in range(NSEQ)], "soo")
    if y_dbg is not None:
        kb.wait("sync", "ys", kb.v("ys"))
        kb.op("sync", "dma_start", out=y_dbg, in_=yD, i=(("tapst", 16),))
        kb.wait("sync", "tapst", kb.v("tapst"))
    for name, (dstap, srcap) in tap_ts.items():
        gates = {"uT": [("sou0", kb.v("sou0")), ("sou1", kb.v("sou1"))],
                 "vT": [("sov0", kb.v("sov0")), ("sov1", kb.v("sov1"))],
                 "weff": [(f"sws{j}", kb.v(f"sws{j}")) for j in range(4)]}[name]
        kb.op("sync", "dma_start", out=dstap, in_=srcap, w=tuple(gates),
              i=(("tapst", 16),))
    fin = [("soo0", kb.v("soo0")), ("soo1", kb.v("soo1"))]
    if tap_ts:
        fin.append(("tapst", kb.v("tapst")))
    for s, val in fin:
        kb.wait("sync", s, val)
        kb.wait("tensor", s, val)

    kb.drain_dma_sems()
    with nc.Block() as block:
        block.sync(lambda e: [f(e) for f in kb.q["sync"]])
        block.tensor(lambda e: [f(e) for f in kb.q["tensor"]])
        block.vector(lambda e: [f(e) for f in kb.q["vector"]])
        block.scalar(lambda e: [f(e) for f in kb.q["scalar"]])
        block.gpsimd(lambda e: [f(e) for f in kb.q["gpsimd"]])
    return nc


def _pretile(w, S, KTn):
    return np.ascontiguousarray(
        w.reshape(S, 128, KTn, 128).transpose(0, 3, 2, 1).reshape(S, 128, KTn * 128))


def host_inputs(inputs):
    f32 = np.float32
    gup = np.stack([_pretile(np.asarray(inputs["L0_ffn_gate_up_proj"][e], f32), 22, 16)
                    for e in range(E)])
    dnp = np.stack([_pretile(np.asarray(inputs["L0_ffn_down_proj"][e], f32), 16, 11)
                    for e in range(E)])
    srow = np.concatenate([np.asarray(inputs[f"L0_attn_s_{k}"], f32).reshape(1)
                           for k in "qkvo"] + [np.zeros(4, f32)]).reshape(1, 8)
    rsT = np.ascontiguousarray(np.concatenate(
        [np.asarray(inputs["L0_ffn_router_weight"], f32).T,
         np.asarray(inputs["L0_ffn_shared_gate_w"], f32).T], axis=1))
    identb = np.eye(128, dtype=np.float32).astype(ml_dtypes.bfloat16)
    inv_freq = 1.0 / (10000.0 ** (np.arange(0, 64, 2, dtype=np.float64) / 64.0))
    fr = np.outer(inv_freq, np.arange(SEQ, dtype=np.float64))
    cos = np.cos(fr).astype(f32)
    sin = np.sin(fr).astype(f32)
    T1 = np.concatenate([cos, sin], axis=0)
    T2 = np.concatenate([sin, cos], axis=0)
    trig = np.concatenate([T1, T2], axis=1).astype(ml_dtypes.bfloat16)
    sel = np.zeros((128, 64), np.float32)
    for j in range(32):
        sel[j, j] = 1.0
        sel[j + 32, j] = 1.0
        sel[64 + j, 32 + j] = -1.0
        sel[64 + 32 + j, 32 + j] = 1.0
    common = dict(
        selrope=sel.astype(ml_dtypes.bfloat16),
        gu_pre=gup, dn_pre=dnp,
        shg_pre=_pretile(np.asarray(inputs["L0_ffn_sh_gate_w"], f32), 44, 16),
        shu_pre=_pretile(np.asarray(inputs["L0_ffn_sh_up_w"], f32), 44, 16),
        shd_pre=_pretile(np.asarray(inputs["L0_ffn_sh_down_w"], f32), 16, 44),
        wq_pre=_pretile(np.asarray(inputs["L0_attn_W_q"], f32), 16, 16),
        wk_pre=_pretile(np.asarray(inputs["L0_attn_W_k"], f32), 8, 16),
        wv_pre=_pretile(np.asarray(inputs["L0_attn_W_v"], f32), 8, 16),
        wo_pre=_pretile(np.asarray(inputs["L0_attn_W_o"], f32), 16, 16),
        a_q=np.asarray(inputs["L0_attn_A_q"], f32),
        a_k=np.asarray(inputs["L0_attn_A_k"], f32),
        a_v=np.asarray(inputs["L0_attn_A_v"], f32),
        a_o=np.asarray(inputs["L0_attn_A_o"], f32),
        bT_q=np.ascontiguousarray(np.asarray(inputs["L0_attn_B_q"], f32).T),
        bT_k=np.ascontiguousarray(np.asarray(inputs["L0_attn_B_k"], f32).T),
        bT_v=np.ascontiguousarray(np.asarray(inputs["L0_attn_B_v"], f32).T),
        bT_o=np.ascontiguousarray(np.asarray(inputs["L0_attn_B_o"], f32).T),
        srow=srow, grow=np.asarray(inputs["L0_attn_q_gain"], f32).reshape(1, NH),
        rsT=rsT, identb=identb, ident9=np.eye(9, dtype=f32), trig=trig,
        maskadd=np.ascontiguousarray(np.triu(np.full((128, 128), -1e30, f32), k=1)),
        maskTb=np.ascontiguousarray(
            np.tril(np.full((128, 128), -1e30, f32), k=-1)).astype(ml_dtypes.bfloat16),
    )
    x = np.asarray(inputs["L0_attn_x"], f32)
    xT = np.ascontiguousarray(x.transpose(0, 2, 1))
    return common, xT


def kernel(**inputs):
    common, xT = host_inputs(inputs)
    nc = build_program()
    in_maps = []
    for c in range(NCORE):
        m = dict(common)
        m["xT"] = xT[2 * c:2 * c + 2]
        in_maps.append(m)
    res = run_bass_kernel_spmd(nc, in_maps, core_ids=list(range(NCORE)))
    out = np.concatenate([r["oT"] for r in res.results], axis=0)
    return np.ascontiguousarray(out.transpose(0, 2, 1))
